# version 16
# baseline (speedup 1.0000x reference)
import contextlib

import numpy as np
import ml_dtypes

import concourse.bass as bass
import concourse.mybir as mybir
from concourse.bass_utils import run_bass_kernel_spmd

F32 = mybir.dt.float32
BF16 = mybir.dt.bfloat16
AF = mybir.ActivationFunctionType
ALU = mybir.AluOpType
AX = mybir.AxisListType
BFnp = ml_dtypes.bfloat16

SEQ, DM, INNER = 2048, 1024, 512
NTC, TCW = 4, 512
NTS = 4
CJ = 8
CI = 4
EPS_PRE = 1e-5
EPS_ATTN = 1e-6
INV_D = 1.0 / DM


def build_nc():
    nc = bass.Bass("TRN2", target_bir_lowering=False,
                   dynamic_dma_scratch_size=8192,
                   detect_race_conditions=False)

    x_d = nc.dram_tensor("x", [SEQ, DM], F32, kind="ExternalInput")
    wq_d = nc.dram_tensor("wq", [DM, INNER], F32, kind="ExternalInput")
    wk_d = nc.dram_tensor("wk", [DM, INNER], F32, kind="ExternalInput")
    wv_d = nc.dram_tensor("wv", [DM, INNER], F32, kind="ExternalInput")
    wo_d = nc.dram_tensor("wo", [INNER, DM], F32, kind="ExternalInput")
    fr_d = nc.dram_tensor("fr", [DM, DM], F32, kind="ExternalInput")
    fk_d = nc.dram_tensor("fk", [DM, DM], F32, kind="ExternalInput")
    fv_d = nc.dram_tensor("fv", [DM, DM], F32, kind="ExternalInput")
    bqc_d = nc.dram_tensor("bqc", [128, 4], F32, kind="ExternalInput")
    bkc_d = nc.dram_tensor("bkc", [128, 4], F32, kind="ExternalInput")
    bvc_d = nc.dram_tensor("bvc", [128, 4], F32, kind="ExternalInput")
    alc_d = nc.dram_tensor("alc", [128, 4], F32, kind="ExternalInput")
    apc_d = nc.dram_tensor("apc", [128, 4], F32, kind="ExternalInput")
    murc_d = nc.dram_tensor("murc", [128, 8], F32, kind="ExternalInput")
    mukc_d = nc.dram_tensor("mukc", [128, 8], F32, kind="ExternalInput")
    grow_d = nc.dram_tensor("grow", [1, DM], F32, kind="ExternalInput")
    brow_d = nc.dram_tensor("brow", [1, DM], F32, kind="ExternalInput")
    borow_d = nc.dram_tensor("borow", [1, DM], F32, kind="ExternalInput")
    eyeb_d = nc.dram_tensor("eyeb", [128, 128], BF16, kind="ExternalInput")
    eyef_d = nc.dram_tensor("eyef", [128, 128], F32, kind="ExternalInput")
    sels_d = nc.dram_tensor("sels", [128, 32], BF16, kind="ExternalInput")
    selb_d = nc.dram_tensor("selb", [8, 512], F32, kind="ExternalInput")
    ones1b_d = nc.dram_tensor("ones1b", [1, 128], BF16, kind="ExternalInput")
    ones1f_d = nc.dram_tensor("ones1f", [1, 128], F32, kind="ExternalInput")
    onescb_d = nc.dram_tensor("onescb", [128, 1], BF16, kind="ExternalInput")
    y_d = nc.dram_tensor("y", [SEQ, DM], F32, kind="ExternalOutput")
    wscr = [nc.dram_tensor(f"wscr{i}", [DM, DM], BF16, kind="Internal")
            for i in range(3)]

    es = contextlib.ExitStack()

    def sb(name, shape, dtype):
        return es.enter_context(nc.sbuf_tensor(name, shape, dtype))

    def ps(name, shape, dtype):
        return es.enter_context(nc.psum_tensor(name, shape, dtype))

    def sem(name):
        return es.enter_context(nc.semaphore(name))

    wq = sb("wq_sb", [128, 4096], BF16)
    wk = sb("wk_sb", [128, 4096], BF16)
    wv = sb("wv_sb", [128, 4096], BF16)
    wo = sb("wo_sb", [128, 4096], BF16)
    wrT = sb("wrT", [128, 8192], BF16)
    wkT = sb("wkT", [128, 8192], BF16)
    wvT = sb("wvT", [128, 8192], BF16)
    xsb = sb("xsb", [128, 8192], F32)
    xnb = sb("xnb", [128, 4096], BF16)
    xT = sb("xT", [128, 8192], BF16)
    bigb = sb("bigb", [128, 8192], BF16)
    qkbuf = sb("qkbuf", [128, 2048], BF16)
    aA = sb("aA", [128, 4096], F32)
    aB = sb("aB", [128, 2048], F32)
    vsb = sb("vsb", [128, 2048], F32)
    xn2T = sb("xn2T", [128, 4104], BF16)
    gb = sb("gb", [128, DM], F32)
    bb = sb("bb", [128, DM], F32)
    sq = sb("sq", [128, 512], F32)
    stats = sb("stats", [128, 152], F32)
    carry = sb("carry", [128, 4], F32)
    decay = sb("decay", [128, 4], F32)
    omd = sb("omd", [128, 4], F32)
    augph = sb("augph", [128, 4], F32)
    s8 = sb("s8", [8, 512], F32)
    arow = s8
    selb = sb("selb_sb", [8, 512], F32)
    stm = sb("stm", [128, 32], F32)
    atm = stm
    mx4 = sb("mx4", [128, 4], F32)
    rc4 = sb("rc4", [128, 4], F32)
    halfc = sb("halfc", [128, 1], F32)
    bffr = sb("bffr", [1, DM], BF16)
    eyeb = sb("eyeb_sb", [128, 128], BF16)
    eyef = sb("eyef_sb", [128, 128], F32)
    sels = sb("sels_sb", [128, 32], BF16)
    ones1b = sb("ones1b_sb", [1, 128], BF16)
    ones1f = sb("ones1f_sb", [1, 128], F32)
    onescb = sb("onescb_sb", [128, 1], BF16)
    bqc = sb("bqc_sb", [128, 4], F32)
    bkc = sb("bkc_sb", [128, 4], F32)
    bvc = sb("bvc_sb", [128, 4], F32)
    alc = sb("alc_sb", [128, 4], F32)
    apc = sb("apc_sb", [128, 4], F32)
    murc = sb("murc_sb", [128, 8], F32)
    mukc = sb("mukc_sb", [128, 8], F32)

    p_mmd = [ps("p_mmA", [128, 1024], F32), ps("p_mmB", [128, 1024], F32)]

    def pmm(bk):
        h = (bk % 2) * 512
        return p_mmd[bk // 2][:, h:h + 512]
    p_tr = [ps("p_tr0", [128, 512], BF16), ps("p_tr1", [128, 512], BF16)]
    p_sm = ps("p_sm", [128, 512], F32)
    p_st = ps("p_st", [128, 512], F32)

    ENGS = ("T", "V", "S", "G", "X")
    esem = {e: sem("sem_" + e) for e in ENGS}
    sem_dc = sem("sem_dc")
    sem_dx = [sem("sem_dx0"), sem("sem_dx1")]
    sem_do = sem("sem_do")
    sem_wq = sem("sem_wq")
    sem_wkv = sem("sem_wkv")
    sem_wo = sem("sem_wo")
    sem_stg = [sem("sem_stg0"), sem("sem_stg1")]

    sched = {e: [] for e in ENGS}
    dma_count = {}

    def emit(e, fn, waits=(), dma_sem=None):
        idx = len(sched[e])
        wl = []
        for w in waits:
            if w is None:
                continue
            if w[0] == "D":
                wl.append(w)
            else:
                pe, pidx = w
                assert pidx < len(sched[pe]) or (pe == e and pidx < idx), \
                    "wait on future entry"
                if pe == e and idx - pidx >= 16:
                    continue
                wl.append(("E", pe, pidx))
        sched[e].append({"fn": fn, "waits": wl, "dma_sem": dma_sem})
        return (e, idx)

    def dma(e, out_ap, in_ap, dsem, waits=()):
        key = dsem.name if hasattr(dsem, "name") else id(dsem)
        dma_count[key] = dma_count.get(key, 0) + 1
        emit(e, lambda eng, o=out_ap, i=in_ap: eng.dma_start(out=o, in_=i),
             waits=waits, dma_sem=dsem)

    def dtok(dsem):
        key = dsem.name if hasattr(dsem, "name") else id(dsem)
        return ("D", dsem, 16 * dma_count.get(key, 0))

    A_ = bass.AP

    for dst, src in ((eyeb, eyeb_d), (eyef, eyef_d), (sels, sels_d),
                     (ones1b, ones1b_d), (ones1f, ones1f_d),
                     (onescb, onescb_d), (bqc, bqc_d), (bkc, bkc_d),
                     (bvc, bvc_d), (alc, alc_d), (apc, apc_d),
                     (murc, murc_d), (mukc, mukc_d)):
        dma("X", dst[:], src.ap(), sem_dc)
    dma("X", selb[:], selb_d.ap(), sem_dc)
    dma("X", vsb[0:1, 1024:2048], grow_d.ap(), sem_dc)
    dma("X", aB[0:1, 0:1024], brow_d.ap(), sem_dc)
    dma("X", aB[0:1, 1024:2048], borow_d.ap(), sem_dc)
    t_consts = dtok(sem_dc)

    def load_x(tc, waitl=None):
        slot = tc % 2
        for ts in range(NTS):
            in_ap = A_(x_d, (tc * TCW + ts * 128) * DM, [[DM, 128], [1, DM]])
            dma("X", xsb[:, slot * 4096 + ts * 1024:
                         slot * 4096 + (ts + 1) * 1024],
                in_ap, sem_dx[slot],
                waits=(waitl[ts],) if waitl is not None else ())
        return dtok(sem_dx[slot])

    t_x = [None] * NTC
    t_x[0] = load_x(0)
    t_x[1] = load_x(1)

    def cast_w_attn(dst, src, dsem):
        in_ap = A_(src, 0, [[INNER, 128], [128 * INNER, CJ], [1, INNER]])
        out_ap = A_(dst, 0, [[4096, 128], [512, CJ], [1, 512]])
        dma("G", out_ap, in_ap, dsem)

    cast_w_attn(wq, wq_d, sem_wq)
    t_wq = dtok(sem_wq)
    cast_w_attn(wk, wk_d, sem_wkv)
    cast_w_attn(wv, wv_d, sem_wkv)
    t_wkv = dtok(sem_wkv)
    dma("G", A_(wo, 0, [[4096, 128], [1024, CI], [1, DM]]),
        A_(wo_d, 0, [[DM, 128], [128 * DM, CI], [1, DM]]), sem_wo)
    t_wo = dtok(sem_wo)

    t_gb = None
    for half in range(2):
        for rowt, roff, dest in ((vsb, 1024, gb), (aB, 0, bb)):
            tmm = emit("T", lambda t, rowt=rowt, roff=roff, h=half: t.matmul(
                p_st[:, 0:512], ones1f[0:1, 0:128],
                rowt[0:1, roff + h * 512: roff + (h + 1) * 512],
                start=True, stop=True), waits=(t_consts, t_gb))
            t_gb = emit("V", lambda v, d=dest, h=half: v.tensor_copy(
                d[:, h * 512:(h + 1) * 512], p_st[:, 0:512]), waits=(tmm,))
    p_st_free = t_gb

    t_bff = None
    for jh in range(2):
        tmm = None
        for ic in range(CI):
            tmm = emit("T", lambda t, ic=ic, jh=jh: t.matmul(
                p_sm[0:1, 0:512], onescb[:, 0:1],
                wo[:, ic * 1024 + jh * 512: ic * 1024 + (jh + 1) * 512],
                start=(ic == 0), stop=(ic == CI - 1)),
                waits=(t_wo, t_consts, t_bff) if ic == 0 else ())
        t_bff = emit("V", lambda v, jh=jh: v.scalar_tensor_tensor(
            out=bffr[0:1, jh * 512:(jh + 1) * 512], in0=p_sm[0:1, 0:512],
            scalar=0.5, in1=aB[0:1, 1024 + jh * 512: 1024 + (jh + 1) * 512],
            op0=ALU.mult, op1=ALU.add), waits=(tmm, t_consts))
    p_sm_free = t_bff

    tdk = emit("S", lambda s: s.activation(stats[:, 72:76], alc[:], AF.Tanh,
                                           scale=0.5), waits=(t_consts,))
    tdk = emit("V", lambda v: v.tensor_scalar(out=decay[:], in0=stats[:, 72:76],
               scalar1=0.5, scalar2=0.5, op0=ALU.mult, op1=ALU.add),
               waits=(tdk,))
    t_decay = emit("V", lambda v: v.tensor_scalar(out=omd[:], in0=stats[:, 72:76],
                   scalar1=-0.5, scalar2=0.5, op0=ALU.mult, op1=ALU.add),
                   waits=(tdk,))
    t_augph = emit("V", lambda v: v.tensor_scalar(out=augph[:], in0=apc[:],
                   scalar1=0.5, scalar2=None, op0=ALU.mult), waits=(t_consts,))
    t_carry0 = emit("V", lambda v: v.memset(carry[:], 0.0))
    t_half = emit("V", lambda v: v.memset(halfc[:], 0.5))
    t_prev0 = emit("V", lambda v: v.memset(A_(xn2T, 0, [[4104, 128], [513, CJ]]),
                                           0.0))

    stage_toks = [None, None]
    trbank_free = {0: None, 1: None}
    t_ffnw = None

    def emit_ffn_weights():
        nonlocal t_ffnw
        casts = []
        for i, w_src in enumerate((fk_d, fr_d, fv_d)):
            dma("G", wscr[i].ap(), w_src.ap(), sem_stg[0])
            casts.append(dtok(sem_stg[0]))
        for i, w_dst in enumerate((wkT, wrT, wvT)):
            for cjc in range(CJ):
                emit("X", lambda x, i=i, cjc=cjc, w_dst=w_dst:
                     x.dma_start(out=w_dst[:, cjc * 1024:(cjc + 1) * 1024],
                                 in_=A_(wscr[i], cjc * 128,
                                        [[DM, DM], [1, 128]]),
                                 transpose=True),
                     waits=(casts[i],), dma_sem=sem_stg[1])
                key = sem_stg[1].name if hasattr(sem_stg[1], "name") else id(sem_stg[1])
                dma_count[key] = dma_count.get(key, 0) + 1
        t_ffnw = dtok(sem_stg[1])

    bank_free = {0: None, 1: None, 2: None, 3: None}
    ST = {
        "scan_prev": None, "prevcol": t_prev0, "xn2T_last": t_prev0,
        "z_done": [None] * NTC, "z_tiles": [None] * NTC,
        "store_all": [None] * (NTC + 1),
        "xT_free": [t_consts, t_consts],
        "vv_last": [None] * NTC, "kk_last": [None] * NTC,
        "smm": [None] * NTC, "rmm": [None] * NTC,
        "t_y0": [None] * NTC, "t_q": [None] * NTC, "t_p": [None] * NTC,
        "t_u": [None] * NTC, "t_evv": [None] * NTC,
        "t_xr": [None] * NTC, "t_xk": [None] * NTC, "t_diff": [None] * NTC,
        "t_xkl": [None] * NTC, "t_xrl": [None] * NTC, "t_d_ev": None,
        "t_x2l": [None] * NTC, "tka_last": [None] * NTC,
        "t_f": [None] * NTC,
        "r_done": [None] * NTC, "xbeta": [None] * NTC,
        "t_x2": [None] * NTC, "t_c2": [None] * NTC,
    }

    def stats_base(tc):
        par = (tc % 2) * 76
        return par + 0, par + 24, par + 48, par + 72

    def xtile(tc, ts):
        return xsb[:, (tc % 2) * 4096 + ts * 1024:
                   (tc % 2) * 4096 + (ts + 1) * 1024]

    def colmath_pair(base, ts0, SCR, eps, wa, wb, split_sum=False):
        def c(k, base=base, ts0=ts0):
            return stats[:, base + k + ts0:base + k + ts0 + 2]
        scr = stats[:, SCR + ts0:SCR + ts0 + 2]
        if split_sum:
            t0 = emit("V", lambda v, c=c: v.tensor_add(
                c(0), c(0), c(4)), waits=tuple(wa))
            wa = (t0,)
        t_a = emit("V", lambda v, c=c, scr=scr: v.tensor_mul(
            scr, c(0), c(0)), waits=tuple(wa))
        t3a = emit("V", lambda v, c=c: v.tensor_add(
            c(8), c(8), c(12)), waits=tuple(wb))
        t_b = emit("V", lambda v, c=c, eps=eps: v.tensor_scalar(
            out=c(8), in0=c(8), scalar1=INV_D, scalar2=eps,
            op0=ALU.mult, op1=ALU.add), waits=(t3a,))
        t_c = emit("V", lambda v, c=c, scr=scr: v.scalar_tensor_tensor(
            out=c(8), in0=scr, scalar=-INV_D * INV_D, in1=c(8),
            op0=ALU.mult, op1=ALU.add), waits=(t_a, t_b))
        t_d = emit("V", lambda v, c=c, scr=scr: v.reciprocal(scr, c(8)),
                   waits=(t_c,))
        t_e = emit("S", lambda s, c=c, scr=scr: s.activation(
            c(16), scr, AF.Sqrt), waits=(t_d,))
        t_f = emit("V", lambda v, c=c: v.scalar_tensor_tensor(
            out=c(20), in0=c(0), scalar=-INV_D, in1=c(16),
            op0=ALU.mult, op1=ALU.mult), waits=(t_e,))
        return t_f

    def ln_stats(tc, base, SCR, eps, src_tile, extra_red=(), extra_sq=()):
        t_cm = []
        for half in range(2):
            tred, tsq = [], []
            for ts in (2 * half, 2 * half + 1):
                tred.append(emit("V", lambda v, a=src_tile(ts),
                                 c=stats[:, base + ts:base + ts + 1]:
                                 v.reduce_sum(c, a, axis=AX.X),
                                 waits=tuple(extra_red)))
                for hh in range(2):
                    tsq.append(emit("S", lambda s, a=src_tile(ts), hh=hh,
                                    c=stats[:, base + 8 + 4 * hh + ts:
                                            base + 9 + 4 * hh + ts]:
                                    s.activation(
                                        sq[:, 0:512],
                                        bass.AP(a.tensor, a.offset + hh * 512,
                                                [[a.ap[0][0], 128], [1, 512]]),
                                        AF.Square, accum_out=c),
                                    waits=tuple(extra_sq)))
            t_cm.append(colmath_pair(base, 2 * half, SCR, eps,
                                     (tred[-1],), (tsq[-1],)))
        return t_cm

    def phase_front_a(tc):
        slot = tc % 2
        ST1, STB, STC, SCR = stats_base(tc)
        if tc >= 2:
            t_x[tc] = load_x(tc, waitl=ST["z_tiles"][tc - 2])

        t_ln1 = ln_stats(tc, ST1, SCR, EPS_PRE, lambda ts: xtile(tc, ts),
                         extra_red=(t_x[tc],), extra_sq=(t_x[tc],))
        t_c = []
        for ts in range(NTS):
            wl = [t_ln1[ts // 2]] if ts % 2 == 0 else []
            if ts == 0 and tc > 0:
                wl.append(ST["t_f"][tc - 1])
            t_c.append(emit("S", lambda s, a=xtile(tc, ts), ts=ts,
                            bi=stats[:, ST1 + 20 + ts:ST1 + 21 + ts],
                            sc=stats[:, ST1 + 16 + ts:ST1 + 17 + ts]:
                            s.activation(xnb[:, ts * 1024:(ts + 1) * 1024], a,
                                         AF.Identity, bias=bi, scale=sc),
                            waits=tuple(wl)))
        t_xbeta = emit("G", lambda g, xoff=slot * 4096: g.tensor_add(
            xsb[:, xoff:xoff + 4096], xsb[:, xoff:xoff + 4096],
            A_(bb, 0, [[DM, 128], [0, NTS], [1, DM]])),
            waits=(t_c[-1], t_ln1[-1], t_gb, t_x[tc]))
        ST["xbeta"][tc] = t_xbeta
        ST["t_c1"] = t_c

    def phase_front_b(tc):
        slot = tc % 2
        xTb = slot * 4096
        t_c = ST["t_c1"]
        t_d_ev = []
        for cj in range(CJ):
            rot = cj % 2
            ttr = None
            for ts in range(NTS):
                wl = [t_c[ts]]
                if ts == 0 and trbank_free[rot] is not None:
                    wl.append(trbank_free[rot])
                ttr = emit("T", lambda t, cj=cj, ts=ts, rot=rot: t.transpose(
                    p_tr[rot][:, ts * 128:(ts + 1) * 128],
                    xnb[:, ts * 1024 + cj * 128: ts * 1024 + (cj + 1) * 128],
                    eyeb[:]), waits=tuple(wl))
            tev = emit("V", lambda v, cj=cj, rot=rot, xTb=xTb: v.tensor_copy(
                xT[:, xTb + cj * 512: xTb + (cj + 1) * 512],
                p_tr[rot][:, 0:512]),
                waits=(ttr, ST["r_done"][tc - 2] if tc >= 2 else None))
            trbank_free[rot] = tev
            t_d_ev.append(tev)
        ST["t_d"] = t_d_ev[-1]
        ST["t_d_ev"] = t_d_ev

        qk_prev_reader = ST["t_y0"][tc - 1] if tc > 0 else None
        aug_prev = ST["t_q"][tc - 1] if tc > 0 else None
        for pi, (wsb, bcol, dstt, doff, twt) in enumerate(
                ((wq, bqc, qkbuf, 0, t_wq), (wk, bkc, xnb, 0, t_wkv),
                 (wv, bvc, vsb, 0, t_wkv))):
            for jc in range(CI):
                bk = 2 + (pi * CI + jc) % 2
                tmm = None
                for cj in range(CJ):
                    wl = [ST["t_d_ev"][cj]]
                    if cj == 0:
                        wl.append(twt)
                        if bank_free[bk] is not None:
                            wl.append(bank_free[bk])
                    tmm = emit("T", lambda t, wsb=wsb, jc=jc, cj=cj, bk=bk,
                               xTb=xTb: t.matmul(
                                   pmm(bk),
                                   wsb[:, cj * 512 + jc * 128:
                                       cj * 512 + (jc + 1) * 128],
                                   xT[:, xTb + cj * 512: xTb + (cj + 1) * 512],
                                   start=(cj == 0), stop=(cj == CJ - 1)),
                               waits=tuple(wl))
                wl = [tmm]
                if pi == 0 and jc == 0 and qk_prev_reader is not None:
                    wl.append(qk_prev_reader)
                if pi == 2 and jc == 0 and aug_prev is not None:
                    wl.append(aug_prev)
                if pi == 1 and jc == 0 and tc > 0:
                    wl.append(ST["tka_last"][tc - 1])
                tev = emit("V", lambda v, dstt=dstt, doff=doff, jc=jc,
                           bk=bk, bcol=bcol: v.tensor_scalar(
                               out=dstt[:, doff + jc * 512:
                                        doff + (jc + 1) * 512],
                               in0=pmm(bk),
                               scalar1=bcol[:, jc:jc + 1], scalar2=None,
                               op0=ALU.add), waits=tuple(wl))
                bank_free[bk] = tev
                if pi == 0:
                    ST["t_evq"] = tev
                elif pi == 1:
                    ST["t_evk"] = tev
                else:
                    ST["t_evv"][tc] = tev

    def phase_tail2_w(tc):
        nonlocal p_sm_free, p_st_free
        if tc > 0:
            ST["prevcol"] = emit("V", lambda v: v.tensor_copy(
                A_(xn2T, 0, [[4104, 128], [513, CJ]]),
                A_(xn2T, 512, [[4104, 128], [513, CJ]])),
                waits=(ST["xn2T_last"],))
        wbank = [p_sm, p_st]
        wfree = [p_sm_free, p_st_free]
        t_w_ev = []
        for cj in range(CJ):
            rot = cj % 2
            ttr = None
            for ts in range(NTS):
                wl = [ST["t_c2"][tc][ts]]
                if ts == 0 and wfree[rot] is not None:
                    wl.append(wfree[rot])
                ttr = emit("T", lambda t, cj=cj, ts=ts, rot=rot: t.transpose(
                    wbank[rot][:, ts * 128:(ts + 1) * 128],
                    aA[:, ts * 1024 + cj * 128: ts * 1024 + (cj + 1) * 128],
                    eyef[:]), waits=tuple(wl))
            tev = emit("S", lambda s, cj=cj, rot=rot: s.activation(
                xn2T[:, cj * 513 + 1: cj * 513 + 513], wbank[rot][:, 0:512],
                AF.Identity), waits=(ttr, ST["prevcol"]))
            wfree[rot] = tev
            t_w_ev.append(tev)
        p_sm_free, p_st_free = wfree
        ST["xn2T_last"] = t_w_ev[-1]
        ST["t_w_ev"] = t_w_ev

    def phase_tail2_mix(tc):
        xTb = (tc % 2) * 4096
        t_w_ev = ST["t_w_ev"]
        t_diff = emit("V", lambda v: v.tensor_sub(
            aA[:, 0:4096], A_(xn2T, 1, [[4104, 128], [513, CJ], [1, 512]]),
            A_(xn2T, 0, [[4104, 128], [513, CJ], [1, 512]])),
            waits=(t_w_ev[-1],))
        ST["t_diff"][tc] = t_diff
        t_xk = []
        for cj in range(CJ):
            wl = [t_diff, t_consts]
            if cj == 0:
                wl.append(ST["kk_last"][tc - 1] if tc > 0 else t_ffnw)
            t_xk.append(emit("V", lambda v, cj=cj: v.scalar_tensor_tensor(
                out=bigb[:, cj * 512:(cj + 1) * 512],
                in0=aA[:, cj * 512:(cj + 1) * 512],
                scalar=mukc[:, cj:cj + 1],
                in1=A_(xn2T, cj * 513, [[4104, 128], [1, 512]]),
                op0=ALU.mult, op1=ALU.add), waits=tuple(wl)))
        ST["t_xk"][tc] = t_xk[-1]
        ST["t_xkl"][tc] = t_xk
        t_xr = []
        for cj in range(CJ):
            wl = [t_diff, t_consts]
            if cj == 0:
                wl.append(ST["t_evv"][tc])
            t_xr.append(emit("V", lambda v, cj=cj, xTb=xTb:
                        v.scalar_tensor_tensor(
                            out=xT[:, xTb + cj * 512: xTb + (cj + 1) * 512],
                            in0=aA[:, cj * 512:(cj + 1) * 512],
                            scalar=murc[:, cj:cj + 1],
                            in1=A_(xn2T, cj * 513, [[4104, 128], [1, 512]]),
                            op0=ALU.mult, op1=ALU.add), waits=tuple(wl)))
        ST["t_xr"][tc] = t_xr[-1]
        ST["t_xrl"][tc] = t_xr

    def phase_mid_a1(tc):
        nonlocal p_sm_free, p_st_free
        vvp = ST["vv_last"][tc - 2] if tc >= 2 else None

        t_f = emit("V", lambda v: v.tensor_mul(
            bigb[:, 4096:6144], xnb[:, 0:2048], qkbuf[:, 0:2048]),
            waits=(ST["t_evq"], ST["t_evk"], vvp))
        ST["t_f"][tc] = t_f

        tmm = None
        for ts in range(NTS):
            for ci in range(CI):
                wl = []
                if ts == 0 and ci == 0:
                    wl = [t_f, t_consts]
                    if p_st_free is not None:
                        wl.append(p_st_free)
                tmm = emit("T", lambda t, ts=ts, ci=ci: t.matmul(
                    p_st[:, ts * 8:(ts + 1) * 8],
                    bigb[:, 4096 + ci * 512 + ts * 128:
                         4096 + ci * 512 + (ts + 1) * 128],
                    sels[:, ci * 8:(ci + 1) * 8],
                    start=(ci == 0), stop=(ci == CI - 1)), waits=tuple(wl))
        ST["smm"][tc] = tmm
        t_j = emit("V", lambda v: v.reduce_max(
            mx4[:], A_(p_st, 0, [[512, 128], [8, 4], [1, 8]]), axis=AX.X,
            negate=True), waits=(tmm,))
        texp = None
        for g in range(4):
            texp = emit("S", lambda s, g=g: s.activation(
                stm[:, g * 8:(g + 1) * 8], p_st[:, g * 8:(g + 1) * 8],
                AF.Exp, bias=mx4[:, g:g + 1]),
                waits=(t_j,) if g == 0 else ())
        p_st_free = texp
        t_j = emit("V", lambda v: v.reduce_sum(
            rc4[:], A_(stm, 0, [[32, 128], [8, 4], [1, 8]]), axis=AX.X),
            waits=(texp,))
        t_j = emit("V", lambda v: v.reciprocal(rc4[:], rc4[:]), waits=(t_j,))
        t_j = emit("V", lambda v: v.tensor_mul(
            A_(stm, 0, [[32, 128], [8, 4], [1, 8]]),
            A_(stm, 0, [[32, 128], [8, 4], [1, 8]]),
            A_(rc4, 0, [[4, 128], [1, 4], [0, 8]])), waits=(t_j,))
        ST["t_jn"] = t_j
        ST["texp"] = texp
        p_st_free = texp

    def phase_mid_a2(tc):
        nonlocal p_sm_free, p_st_free
        t_j = ST["t_jn"]
        ttr = None
        for ts in range(NTS):
            wl = []
            if ts == 0:
                wl = [t_j]
                if p_sm_free is not None:
                    wl.append(p_sm_free)
            ttr = emit("T", lambda t, ts=ts: t.transpose(
                p_sm[0:8, ts * 128:(ts + 1) * 128], stm[:, ts * 8:(ts + 1) * 8],
                eyef[:]), waits=tuple(wl))
        p_sm_free = emit("V", lambda v: v.tensor_copy(arow[:],
                                                      p_sm[0:8, 0:512]),
                         waits=(ttr,))
        t_l = p_sm_free

        t_u = []
        for ci in range(CI):
            bk = 2 + ci % 2
            wl = [t_l, t_consts]
            if bank_free[bk] is not None:
                wl.append(bank_free[bk])
            tmm = emit("T", lambda t, ci=ci, bk=bk: t.matmul(
                pmm(bk), selb[0:8, ci * 128:(ci + 1) * 128],
                arow[0:8, 0:512], start=True, stop=True), waits=tuple(wl))
            wl = [tmm, ST["t_evv"][tc], t_decay]
            if ci == 0 and tc > 0:
                wl.append(ST["t_q"][tc - 1])
            tn = emit("V", lambda v, ci=ci, bk=bk: v.scalar_tensor_tensor(
                out=aB[:, ci * 512:(ci + 1) * 512], in0=pmm(bk),
                scalar=omd[:, ci:ci + 1], in1=vsb[:, ci * 512:(ci + 1) * 512],
                op0=ALU.mult, op1=ALU.mult), waits=tuple(wl))
            bank_free[bk] = tn
            t_u.append(tn)
        ST["t_u"][tc] = t_u[-1]

        t_scan = []
        for ci in range(CI):
            wl = [t_u[ci]]
            if tc == 0 and ci == 0:
                wl.append(t_carry0)
            t_scan.append(emit("V", lambda v, ci=ci: v.tensor_tensor_scan(
                out=aB[:, ci * 512:(ci + 1) * 512],
                data0=A_(decay, ci, [[4, 128], [0, 512]]),
                data1=aB[:, ci * 512:(ci + 1) * 512],
                initial=carry[:, ci:ci + 1], op0=ALU.mult, op1=ALU.add),
                waits=tuple(wl)))
        ST["scan_prev"] = emit("V", lambda v: v.tensor_copy(
            carry[:, 0:4], A_(aB, 511, [[2048, 128], [512, 4]])),
            waits=(t_scan[-1],))

        ST["t_ul"] = t_u
        t_y0 = emit("V", lambda v: v.tensor_mul(
            aB[:, 0:2048], aB[:, 0:2048], qkbuf[:, 0:2048]),
            waits=(ST["scan_prev"],))
        ST["t_y0"][tc] = t_y0
    def phase_mid_b(tc):
        t_u = ST["t_ul"]
        t_p = []
        for ci in range(CI):
            t_p.append(emit("S", lambda s, ci=ci: s.activation(
                vsb[:, ci * 512:(ci + 1) * 512],
                qkbuf[:, ci * 512:(ci + 1) * 512],
                AF.Tanh, scale=augph[:, ci:ci + 1]),
                waits=(t_augph, t_u[ci])))
        ST["t_p"][tc] = t_p[-1]
        t_q = emit("V", lambda v: v.scalar_tensor_tensor(
            out=bigb[:, 6144:8192], in0=vsb[:, 0:2048], scalar=0.5,
            in1=aB[:, 0:2048], op0=ALU.mult, op1=ALU.add),
            waits=(ST["t_p"][tc], ST["t_y0"][tc],
                   ST["vv_last"][tc - 1] if tc > 0 else None))
        ST["t_q"][tc] = t_q

    def phase_mid_r(tc):
        ST1, STB, STC, SCR = stats_base(tc)
        t_q = ST["t_q"][tc]
        t_oev = []
        for ts in range(NTS):
            for jh in range(2):
                bk = (ts * 2 + jh) % 4
                tmm = None
                for ic in range(CI):
                    wl = []
                    if ic == 0:
                        wl = [t_q, t_bff]
                        if bank_free[bk] is not None:
                            wl.append(bank_free[bk])
                    tmm = emit("T", lambda t, ts=ts, jh=jh, ic=ic, bk=bk:
                               t.matmul(pmm(bk),
                                        bigb[:, 6144 + ic * 512 + ts * 128:
                                             6144 + ic * 512 + (ts + 1) * 128],
                                        wo[:, ic * 1024 + jh * 512:
                                           ic * 1024 + (jh + 1) * 512],
                                        start=(ic == 0), stop=False),
                               waits=tuple(wl))
                tmm = emit("T", lambda t, jh=jh, bk=bk: t.matmul(
                    pmm(bk), ones1b[0:1, 0:128],
                    bffr[0:1, jh * 512:(jh + 1) * 512], start=False, stop=True))
            wl = [tmm]
            if ts == 0 and ST["store_all"][tc - 1] is not None:
                wl.append(ST["store_all"][tc - 1])
            tev = emit("S", lambda s, ts=ts,
                       c=stats[:, STB + ts:STB + ts + 1]:
                       s.activation(
                           aA[:, ts * 1024:(ts + 1) * 1024],
                           p_mmd[ts % 2][:, 0:1024], AF.Identity,
                           accum_out=c),
                       waits=tuple(wl))
            bank_free[(ts * 2) % 4] = tev
            bank_free[(ts * 2 + 1) % 4] = tev
            t_oev.append(tev)
        ST["rmm"][tc] = tmm
        ST["t_oev"] = t_oev

    def phase_tail1_s1(tc, half):
        ST1, STB, STC, SCR = stats_base(tc)
        t_oev = ST["t_oev"]
        if half == 0:
            ST["t_c2"][tc] = [None] * NTS
            ST["t_x2l"][tc] = [None] * NTS
            ST["t_t1"] = [None] * NTS
        p0 = 2 * half
        tsq = []
        for ts in (p0, p0 + 1):
            for hh in range(2):
                tsq.append(emit("S", lambda s, ts=ts, hh=hh,
                                c=stats[:, STB + 8 + 4 * hh + ts:
                                        STB + 9 + 4 * hh + ts]:
                                s.activation(
                                    sq[:, 0:512],
                                    aA[:, ts * 1024 + hh * 512:
                                       ts * 1024 + (hh + 1) * 512],
                                    AF.Square, accum_out=c),
                                waits=(t_oev[ts],)))
        t_lnb = colmath_pair(STB, p0, SCR, EPS_ATTN,
                             (t_oev[p0 + 1],), (tsq[-1],))
        for ts in (p0, p0 + 1):
            ST["t_t1"][ts] = emit("S", lambda s, ts=ts,
                                  bi=stats[:, STB + 20 + ts:STB + 21 + ts],
                                  sc=stats[:, STB + 16 + ts:STB + 17 + ts]:
                                  s.activation(
                                      aA[:, ts * 1024:(ts + 1) * 1024],
                                      aA[:, ts * 1024:(ts + 1) * 1024],
                                      AF.Identity, bias=bi, scale=sc),
                                  waits=(t_lnb,) if ts == p0 else ())

    def phase_tail1_s2(tc, half):
        ST1, STB, STC, SCR = stats_base(tc)
        t_c2 = ST["t_c2"][tc]
        x2_all = ST["t_x2l"][tc]
        t_x2 = None
        p0 = 2 * half
        x2l = {}
        for ts in (p0, p0 + 1):
            tg = emit("V", lambda v, ts=ts: v.tensor_mul(
                aA[:, ts * 1024:(ts + 1) * 1024],
                aA[:, ts * 1024:(ts + 1) * 1024], gb[:, 0:DM]),
                waits=(ST["t_t1"][ts], t_gb))
            t_x2 = emit("V", lambda v, a=xtile(tc, ts), ts=ts,
                        c=stats[:, STC + ts:STC + ts + 1]:
                        v.scalar_tensor_tensor(
                            out=a, in0=aA[:, ts * 1024:(ts + 1) * 1024],
                            scalar=1.0, in1=a, op0=ALU.mult, op1=ALU.add,
                            accum_out=c),
                        waits=(tg, ST["xbeta"][tc]))
            x2l[ts] = t_x2
            x2_all[ts] = t_x2
        tsq = []
        for ts in (p0, p0 + 1):
            for hh in range(2):
                tsq.append(emit("S", lambda s, ts=ts, hh=hh,
                                a=xtile(tc, ts),
                                c=stats[:, STC + 8 + 4 * hh + ts:
                                        STC + 9 + 4 * hh + ts]:
                                s.activation(
                                    sq[:, 0:512],
                                    bass.AP(a.tensor, a.offset + hh * 512,
                                            [[a.ap[0][0], 128], [1, 512]]),
                                    AF.Square, accum_out=c),
                                waits=(x2l[ts],)))
        t_lnc = colmath_pair(STC, p0, SCR, EPS_PRE,
                             (t_x2,), (tsq[-1],))
        for ts in (p0, p0 + 1):
            t_c2[ts] = emit("S", lambda s, a=xtile(tc, ts), ts=ts,
                            bi=stats[:, STC + 20 + ts:STC + 21 + ts],
                            sc=stats[:, STC + 16 + ts:STC + 17 + ts]:
                            s.activation(
                                aA[:, ts * 1024:(ts + 1) * 1024], a,
                                AF.Identity, bias=bi, scale=sc),
                            waits=(t_lnc,) if ts == p0 else ())
        ST["t_x2"][tc] = t_x2

    def phase_ffn(tc):
        xTb = (tc % 2) * 4096
        t_ka = []
        for djp in range(0, CJ, 2):
            tmm = None
            for dj in (djp, djp + 1):
                bk = dj % 2
                for cjc in range(CJ):
                    wl = [ST["t_xkl"][tc][cjc]]
                    if cjc == 0:
                        wl.append(t_ffnw)
                        if bank_free[bk] is not None:
                            wl.append(bank_free[bk])
                    tmm = emit("T", lambda t, dj=dj, cjc=cjc, bk=bk: t.matmul(
                        pmm(bk),
                        wkT[:, cjc * 1024 + dj * 128:
                            cjc * 1024 + (dj + 1) * 128],
                        bigb[:, cjc * 512:(cjc + 1) * 512],
                        start=(cjc == 0), stop=(cjc == CJ - 1)),
                        waits=tuple(wl))
            trl = emit("V", lambda v, djp=djp: v.tensor_scalar_max(
                xnb[:, 0:1024], p_mmd[0][:, 0:1024], 0.0),
                waits=(tmm, ST["t_f"][min(tc + 1, NTC - 1)]))
            wl = [trl]
            if djp < 4:
                wl.append(ST["smm"][min(tc + 1, NTC - 1)])
            else:
                wl.append(ST["rmm"][tc])
            tka = emit("S", lambda s, djp=djp: s.activation(
                bigb[:, 4096 + djp * 512: 4096 + (djp + 2) * 512],
                xnb[:, 0:1024], AF.Square), waits=tuple(wl))
            bank_free[0] = tka
            bank_free[1] = tka
            t_ka.append(tka)
        kk_mm_last = tmm
        ST["tka_last"][tc] = t_ka[-1]

        ST["kk_last"][tc] = kk_mm_last

        last_tz = None
        tz3 = None
        z_tiles = []
        t_sg = []
        for ts in range(NTS):
            tmm = None
            for dh in range(2):
                bk = 2 + (ts * 2 + dh) % 2
                for cjc in range(CJ):
                    wl = [ST["t_xrl"][tc][cjc]]
                    if cjc == 0:
                        wl.append(t_ffnw)
                        if bank_free[bk] is not None:
                            wl.append(bank_free[bk])
                    tmm = emit("T", lambda t, ts=ts, dh=dh, cjc=cjc, bk=bk,
                               xTb=xTb: t.matmul(
                                   pmm(bk),
                                   xT[:, xTb + cjc * 512 + ts * 128:
                                      xTb + cjc * 512 + (ts + 1) * 128],
                                   wrT[:, cjc * 1024 + dh * 512:
                                       cjc * 1024 + (dh + 1) * 512],
                                   start=(cjc == 0), stop=(cjc == CJ - 1)),
                               waits=tuple(wl))
            wl = [tmm, ST["t_xk"][tc], ST["t_diff"][tc],
                  ST["t_c2"][tc][-1]]
            ttr2 = emit("S", lambda s, ts=ts: s.activation(
                aA[:, ts * 1024:(ts + 1) * 1024],
                p_mmd[1][:, 0:1024], AF.Tanh, scale=0.5), waits=tuple(wl))
            bank_free[2] = ttr2
            bank_free[3] = ttr2
            sg = emit("S", lambda s, ts=ts: s.activation(
                aA[:, ts * 1024:(ts + 1) * 1024],
                aA[:, ts * 1024:(ts + 1) * 1024],
                AF.Identity, bias=halfc[:, 0:1], scale=halfc[:, 0:1]),
                waits=(ttr2, t_half))
            t_sg.append(sg)
            tmm = None
            for ih in range(2):
                bk = (ts * 2 + ih) % 2
                for dj in range(CJ):
                    wl = []
                    if dj == 0:
                        wl = [t_ka[-1], t_ffnw]
                        if bank_free[bk] is not None:
                            wl.append(bank_free[bk])
                    tmm = emit("T", lambda t, ts=ts, ih=ih, dj=dj, bk=bk:
                               t.matmul(
                                   pmm(bk),
                                   bigb[:, 4096 + dj * 512 + ts * 128:
                                        4096 + dj * 512 + (ts + 1) * 128],
                                   wvT[:, dj * 1024 + ih * 512:
                                       dj * 1024 + (ih + 1) * 512],
                                   start=(dj == 0), stop=(dj == CJ - 1)),
                               waits=tuple(wl))
            ST["vv_last"][tc] = tmm
            tz = emit("V", lambda v, ts=ts: v.tensor_mul(
                aA[:, ts * 1024:(ts + 1) * 1024],
                p_mmd[0][:, 0:1024],
                aA[:, ts * 1024:(ts + 1) * 1024]),
                waits=(tmm, sg))
            bank_free[0] = tz
            bank_free[1] = tz
            last_tz = tz
            tz3 = emit("V" if ts < 2 else "G",
                       lambda e, ts=ts, a=xtile(tc, ts): e.tensor_add(
                aA[:, ts * 1024:(ts + 1) * 1024],
                aA[:, ts * 1024:(ts + 1) * 1024], a),
                waits=(last_tz, ST["t_x2l"][tc][ts]))
            z_tiles.append(tz3)
            out_ap = A_(y_d, (tc * TCW + ts * 128) * DM, [[DM, 128], [1, DM]])
            dma("X", out_ap, aA[:, ts * 1024:(ts + 1) * 1024], sem_do,
                waits=(tz3,))
        ST["r_done"][tc] = t_sg[-1]
        ST["z_done"][tc] = tz3
        ST["z_tiles"][tc] = z_tiles
        ST["store_all"][tc] = dtok(sem_do)

    phase_front_a(0)
    emit_ffn_weights()
    phase_front_b(0)
    phase_mid_a1(0)
    phase_mid_a2(0)
    for tc in range(NTC):
        if tc > 0:
            phase_tail2_mix(tc - 1)
            phase_ffn(tc - 1)
        phase_mid_b(tc)
        phase_mid_r(tc)
        if tc + 1 < NTC:
            phase_front_a(tc + 1)
            phase_front_b(tc + 1)
        phase_tail1_s1(tc, 0)
        phase_tail1_s1(tc, 1)
        if tc + 1 < NTC:
            phase_mid_a1(tc + 1)
        phase_tail1_s2(tc, 0)
        phase_tail1_s2(tc, 1)
        phase_tail2_w(tc)
        if tc + 1 < NTC:
            phase_mid_a2(tc + 1)
    phase_tail2_mix(NTC - 1)
    phase_ffn(NTC - 1)

    emit("X", None, waits=(dtok(sem_do),))

    marks = {e: set() for e in ENGS}
    for e in ENGS:
        for ent in sched[e]:
            for w in ent["waits"]:
                if w[0] == "E":
                    marks[w[1]].add(w[2])
    mile = {}
    for e in ENGS:
        mile[e] = {idx: n + 1 for n, idx in enumerate(sorted(marks[e]))}

    def runner(e):
        def run(eng):
            waited = {}
            for idx, ent in enumerate(sched[e]):
                for w in ent["waits"]:
                    if w[0] == "E":
                        _, pe, pidx = w
                        val = mile[pe][pidx]
                        key = "E" + pe
                        if waited.get(key, 0) < val:
                            eng.wait_ge(esem[pe], val)
                            waited[key] = val
                    else:
                        _, dsem, val = w
                        key = dsem.name if hasattr(dsem, "name") else id(dsem)
                        if waited.get(("D", key), 0) < val:
                            eng.wait_ge(dsem, val)
                            waited[("D", key)] = val
                if ent["fn"] is None:
                    continue
                inst = ent["fn"](eng)
                if ent["dma_sem"] is not None:
                    inst = inst.then_inc(ent["dma_sem"], 16)
                if idx in mile[e]:
                    inst.then_inc(esem[e], 1)
        return run

    with es:
        with nc.Block() as blk:
            blk.tensor(runner("T"))
            blk.vector(runner("V"))
            blk.scalar(runner("S"))
            blk.gpsimd(runner("G"))
            blk.sync(runner("X"))
    return nc


_NC_CACHE = None


def kernel(L0_attn_x, L0_attn_Wq, L0_attn_bq, L0_attn_Wk, L0_attn_bk,
           L0_attn_Wv, L0_attn_bv, L0_attn_Wo, L0_attn_bo, L0_attn_alpha,
           L0_attn_aug_param, L0_attn_ln_scale, L0_attn_ln_bias,
           L0_ffn_W_r, L0_ffn_W_k, L0_ffn_W_v, L0_ffn_mu_r, L0_ffn_mu_k):
    global _NC_CACHE
    if _NC_CACHE is None:
        _NC_CACHE = build_nc()
    nc = _NC_CACHE

    c = np.ascontiguousarray
    f32 = np.float32

    def cols(vec, k):
        return c(np.asarray(vec, f32).reshape(k, 128).T)

    base = {
        "wq": c(np.asarray(L0_attn_Wq, f32)),
        "wk": c(np.asarray(L0_attn_Wk, f32)),
        "wv": c(np.asarray(L0_attn_Wv, f32)),
        "wo": c(np.asarray(L0_attn_Wo, f32)),
        "fr": c(np.asarray(L0_ffn_W_r, f32)),
        "fk": c(np.asarray(L0_ffn_W_k, f32)),
        "fv": c(np.asarray(L0_ffn_W_v, f32)),
        "bqc": cols(L0_attn_bq, 4),
        "bkc": cols(L0_attn_bk, 4),
        "bvc": cols(L0_attn_bv, 4),
        "alc": cols(np.asarray(L0_attn_alpha, f32).reshape(512), 4),
        "apc": cols(np.asarray(L0_attn_aug_param, f32).reshape(512), 4),
        "murc": cols(L0_ffn_mu_r, 8),
        "mukc": cols(L0_ffn_mu_k, 8),
        "grow": c(np.asarray(L0_attn_ln_scale, f32).reshape(1, DM)),
        "brow": c(np.asarray(L0_attn_ln_bias, f32).reshape(1, DM)),
        "borow": c(np.asarray(L0_attn_bo, f32).reshape(1, DM)),
        "eyeb": np.eye(128, dtype=BFnp),
        "eyef": np.eye(128, dtype=f32),
        "ones1b": np.ones((1, 128), BFnp),
        "ones1f": np.ones((1, 128), f32),
        "onescb": np.ones((128, 1), BFnp),
    }
    sels = np.zeros((128, 32), f32)
    selb = np.zeros((8, 512), f32)
    for ci in range(4):
        for p in range(128):
            h = 2 * ci + (1 if p >= 64 else 0)
            sels[p, ci * 8 + h] = 0.125
            selb[h, ci * 128 + p] = 1.0
    base["sels"] = sels.astype(BFnp)
    base["selb"] = selb

    xs = np.asarray(L0_attn_x, f32)
    in_maps = []
    for b in range(8):
        m = dict(base)
        m["x"] = c(xs[b])
        in_maps.append(m)

    res = run_bass_kernel_spmd(nc, in_maps, core_ids=list(range(8)))
    out = np.stack([np.asarray(r["y"]) for r in res.results], axis=0)
    return out.astype(np.float32)


# version 17
# speedup vs baseline: 1.0500x; 1.0500x over previous
import contextlib

import numpy as np
import ml_dtypes

import concourse.bass as bass
import concourse.mybir as mybir
from concourse.bass_utils import run_bass_kernel_spmd

F32 = mybir.dt.float32
BF16 = mybir.dt.bfloat16
AF = mybir.ActivationFunctionType
ALU = mybir.AluOpType
AX = mybir.AxisListType
BFnp = ml_dtypes.bfloat16

SEQ, DM, INNER = 2048, 1024, 512
NTC, TCW = 4, 512
NTS = 4
CJ = 8
CI = 4
EPS_PRE = 1e-5
EPS_ATTN = 1e-6
INV_D = 1.0 / DM


def build_nc():
    nc = bass.Bass("TRN2", target_bir_lowering=False,
                   dynamic_dma_scratch_size=8192,
                   detect_race_conditions=False)

    x_d = nc.dram_tensor("x", [SEQ, DM], F32, kind="ExternalInput")
    wq_d = nc.dram_tensor("wq", [DM, INNER], F32, kind="ExternalInput")
    wk_d = nc.dram_tensor("wk", [DM, INNER], F32, kind="ExternalInput")
    wv_d = nc.dram_tensor("wv", [DM, INNER], F32, kind="ExternalInput")
    wo_d = nc.dram_tensor("wo", [INNER, DM], F32, kind="ExternalInput")
    fr_d = nc.dram_tensor("fr", [DM, DM], F32, kind="ExternalInput")
    fk_d = nc.dram_tensor("fk", [DM, DM], F32, kind="ExternalInput")
    fv_d = nc.dram_tensor("fv", [DM, DM], F32, kind="ExternalInput")
    bqc_d = nc.dram_tensor("bqc", [128, 4], F32, kind="ExternalInput")
    bkc_d = nc.dram_tensor("bkc", [128, 4], F32, kind="ExternalInput")
    bvc_d = nc.dram_tensor("bvc", [128, 4], F32, kind="ExternalInput")
    alc_d = nc.dram_tensor("alc", [128, 4], F32, kind="ExternalInput")
    apc_d = nc.dram_tensor("apc", [128, 4], F32, kind="ExternalInput")
    murc_d = nc.dram_tensor("murc", [128, 8], F32, kind="ExternalInput")
    mukc_d = nc.dram_tensor("mukc", [128, 8], F32, kind="ExternalInput")
    grow_d = nc.dram_tensor("grow", [1, DM], F32, kind="ExternalInput")
    brow_d = nc.dram_tensor("brow", [1, DM], F32, kind="ExternalInput")
    borow_d = nc.dram_tensor("borow", [1, DM], F32, kind="ExternalInput")
    eyeb_d = nc.dram_tensor("eyeb", [128, 128], BF16, kind="ExternalInput")
    eyef_d = nc.dram_tensor("eyef", [128, 128], F32, kind="ExternalInput")
    sels_d = nc.dram_tensor("sels", [128, 32], BF16, kind="ExternalInput")
    selb_d = nc.dram_tensor("selb", [8, 512], F32, kind="ExternalInput")
    ones1b_d = nc.dram_tensor("ones1b", [1, 128], BF16, kind="ExternalInput")
    ones1f_d = nc.dram_tensor("ones1f", [1, 128], F32, kind="ExternalInput")
    onescb_d = nc.dram_tensor("onescb", [128, 1], BF16, kind="ExternalInput")
    y_d = nc.dram_tensor("y", [SEQ, DM], F32, kind="ExternalOutput")
    wscr = [nc.dram_tensor(f"wscr{i}", [DM, DM], BF16, kind="Internal")
            for i in range(3)]

    es = contextlib.ExitStack()

    def sb(name, shape, dtype):
        return es.enter_context(nc.sbuf_tensor(name, shape, dtype))

    def ps(name, shape, dtype):
        return es.enter_context(nc.psum_tensor(name, shape, dtype))

    def sem(name):
        return es.enter_context(nc.semaphore(name))

    wq = sb("wq_sb", [128, 4096], BF16)
    wk = sb("wk_sb", [128, 4096], BF16)
    wv = sb("wv_sb", [128, 4096], BF16)
    wo = sb("wo_sb", [128, 4096], BF16)
    wrT = sb("wrT", [128, 8192], BF16)
    wkT = sb("wkT", [128, 8192], BF16)
    wvT = sb("wvT", [128, 8192], BF16)
    xsb = sb("xsb", [128, 8192], F32)
    xnb = sb("xnb", [128, 4096], BF16)
    xT = sb("xT", [128, 8192], BF16)
    bigb = sb("bigb", [128, 8192], BF16)
    qkbuf = sb("qkbuf", [128, 2048], BF16)
    aA = sb("aA", [128, 4096], F32)
    aB = sb("aB", [128, 2048], F32)
    vsb = sb("vsb", [128, 2048], F32)
    xn2T = sb("xn2T", [128, 4104], BF16)
    gb = sb("gb", [128, DM], F32)
    bb = sb("bb", [128, DM], F32)
    sq = sb("sq", [128, 512], F32)
    stats = sb("stats", [128, 152], F32)
    carry = sb("carry", [128, 4], F32)
    decay = sb("decay", [128, 4], F32)
    omd = sb("omd", [128, 4], F32)
    augph = sb("augph", [128, 4], F32)
    s8 = sb("s8", [8, 512], F32)
    arow = s8
    selb = sb("selb_sb", [8, 512], F32)
    stm = sb("stm", [128, 32], F32)
    atm = stm
    mx4 = sb("mx4", [128, 4], F32)
    rc4 = sb("rc4", [128, 4], F32)
    halfc = sb("halfc", [128, 1], F32)
    bffr = sb("bffr", [1, DM], BF16)
    eyeb = sb("eyeb_sb", [128, 128], BF16)
    eyef = sb("eyef_sb", [128, 128], F32)
    sels = sb("sels_sb", [128, 32], BF16)
    ones1b = sb("ones1b_sb", [1, 128], BF16)
    ones1f = sb("ones1f_sb", [1, 128], F32)
    onescb = sb("onescb_sb", [128, 1], BF16)
    bqc = sb("bqc_sb", [128, 4], F32)
    bkc = sb("bkc_sb", [128, 4], F32)
    bvc = sb("bvc_sb", [128, 4], F32)
    alc = sb("alc_sb", [128, 4], F32)
    apc = sb("apc_sb", [128, 4], F32)
    murc = sb("murc_sb", [128, 8], F32)
    mukc = sb("mukc_sb", [128, 8], F32)

    p_mm = [ps("p_mm0", [128, 512], F32), ps("p_mm1", [128, 512], F32),
            ps("p_mm2", [128, 512], F32), ps("p_mm3", [128, 512], F32)]
    p_tr = [ps("p_tr0", [128, 512], BF16), ps("p_tr1", [128, 512], BF16)]
    p_sm = ps("p_sm", [128, 512], F32)
    p_st = ps("p_st", [128, 512], F32)

    ENGS = ("T", "V", "S", "G", "X")
    esem = {e: sem("sem_" + e) for e in ENGS}
    sem_dc = sem("sem_dc")
    sem_dx = [sem("sem_dx0"), sem("sem_dx1")]
    sem_do = sem("sem_do")
    sem_wq = sem("sem_wq")
    sem_wkv = sem("sem_wkv")
    sem_wo = sem("sem_wo")
    sem_stg = [sem("sem_stg0"), sem("sem_stg1")]

    sched = {e: [] for e in ENGS}
    dma_count = {}

    def emit(e, fn, waits=(), dma_sem=None):
        idx = len(sched[e])
        wl = []
        for w in waits:
            if w is None:
                continue
            if w[0] == "D":
                wl.append(w)
            else:
                pe, pidx = w
                assert pidx < len(sched[pe]) or (pe == e and pidx < idx), \
                    "wait on future entry"
                if pe == e and idx - pidx >= 16:
                    continue
                wl.append(("E", pe, pidx))
        sched[e].append({"fn": fn, "waits": wl, "dma_sem": dma_sem})
        return (e, idx)

    def dma(e, out_ap, in_ap, dsem, waits=()):
        key = dsem.name if hasattr(dsem, "name") else id(dsem)
        dma_count[key] = dma_count.get(key, 0) + 1
        emit(e, lambda eng, o=out_ap, i=in_ap: eng.dma_start(out=o, in_=i),
             waits=waits, dma_sem=dsem)

    def dtok(dsem):
        key = dsem.name if hasattr(dsem, "name") else id(dsem)
        return ("D", dsem, 16 * dma_count.get(key, 0))

    A_ = bass.AP

    for dst, src in ((eyeb, eyeb_d), (eyef, eyef_d), (sels, sels_d),
                     (ones1b, ones1b_d), (ones1f, ones1f_d),
                     (onescb, onescb_d), (bqc, bqc_d), (bkc, bkc_d),
                     (bvc, bvc_d), (alc, alc_d), (apc, apc_d),
                     (murc, murc_d), (mukc, mukc_d)):
        dma("X", dst[:], src.ap(), sem_dc)
    dma("X", selb[:], selb_d.ap(), sem_dc)
    dma("X", vsb[0:1, 1024:2048], grow_d.ap(), sem_dc)
    dma("X", aB[0:1, 0:1024], brow_d.ap(), sem_dc)
    dma("X", aB[0:1, 1024:2048], borow_d.ap(), sem_dc)
    t_consts = dtok(sem_dc)

    def load_x(tc, waitl=None):
        slot = tc % 2
        for ts in range(NTS):
            in_ap = A_(x_d, (tc * TCW + ts * 128) * DM, [[DM, 128], [1, DM]])
            dma("X", xsb[:, slot * 4096 + ts * 1024:
                         slot * 4096 + (ts + 1) * 1024],
                in_ap, sem_dx[slot],
                waits=(waitl[ts],) if waitl is not None else ())
        return dtok(sem_dx[slot])

    t_x = [None] * NTC
    t_x[0] = load_x(0)
    t_x[1] = load_x(1)

    def cast_w_attn(dst, src, dsem):
        in_ap = A_(src, 0, [[INNER, 128], [128 * INNER, CJ], [1, INNER]])
        out_ap = A_(dst, 0, [[4096, 128], [512, CJ], [1, 512]])
        dma("G", out_ap, in_ap, dsem)

    cast_w_attn(wq, wq_d, sem_wq)
    t_wq = dtok(sem_wq)
    cast_w_attn(wk, wk_d, sem_wkv)
    cast_w_attn(wv, wv_d, sem_wkv)
    t_wkv = dtok(sem_wkv)
    dma("G", A_(wo, 0, [[4096, 128], [1024, CI], [1, DM]]),
        A_(wo_d, 0, [[DM, 128], [128 * DM, CI], [1, DM]]), sem_wo)
    t_wo = dtok(sem_wo)

    t_gb = None
    for half in range(2):
        for rowt, roff, dest in ((vsb, 1024, gb), (aB, 0, bb)):
            tmm = emit("T", lambda t, rowt=rowt, roff=roff, h=half: t.matmul(
                p_st[:, 0:512], ones1f[0:1, 0:128],
                rowt[0:1, roff + h * 512: roff + (h + 1) * 512],
                start=True, stop=True), waits=(t_consts, t_gb))
            t_gb = emit("V", lambda v, d=dest, h=half: v.tensor_copy(
                d[:, h * 512:(h + 1) * 512], p_st[:, 0:512]), waits=(tmm,))
    p_st_free = t_gb

    t_bff = None
    for jh in range(2):
        tmm = None
        for ic in range(CI):
            tmm = emit("T", lambda t, ic=ic, jh=jh: t.matmul(
                p_sm[0:1, 0:512], onescb[:, 0:1],
                wo[:, ic * 1024 + jh * 512: ic * 1024 + (jh + 1) * 512],
                start=(ic == 0), stop=(ic == CI - 1)),
                waits=(t_wo, t_consts, t_bff) if ic == 0 else ())
        t_bff = emit("V", lambda v, jh=jh: v.scalar_tensor_tensor(
            out=bffr[0:1, jh * 512:(jh + 1) * 512], in0=p_sm[0:1, 0:512],
            scalar=0.5, in1=aB[0:1, 1024 + jh * 512: 1024 + (jh + 1) * 512],
            op0=ALU.mult, op1=ALU.add), waits=(tmm, t_consts))
    p_sm_free = t_bff

    tdk = emit("S", lambda s: s.activation(stats[:, 72:76], alc[:], AF.Tanh,
                                           scale=0.5), waits=(t_consts,))
    tdk = emit("V", lambda v: v.tensor_scalar(out=decay[:], in0=stats[:, 72:76],
               scalar1=0.5, scalar2=0.5, op0=ALU.mult, op1=ALU.add),
               waits=(tdk,))
    t_decay = emit("V", lambda v: v.tensor_scalar(out=omd[:], in0=stats[:, 72:76],
                   scalar1=-0.5, scalar2=0.5, op0=ALU.mult, op1=ALU.add),
                   waits=(tdk,))
    t_augph = emit("V", lambda v: v.tensor_scalar(out=augph[:], in0=apc[:],
                   scalar1=0.5, scalar2=None, op0=ALU.mult), waits=(t_consts,))
    t_carry0 = emit("V", lambda v: v.memset(carry[:], 0.0))
    t_half = emit("V", lambda v: v.memset(halfc[:], 0.5))
    t_prev0 = emit("V", lambda v: v.memset(A_(xn2T, 0, [[4104, 128], [513, CJ]]),
                                           0.0))

    stage_toks = [None, None]
    trbank_free = {0: None, 1: None}
    t_ffnw = None

    def emit_ffn_weights():
        nonlocal t_ffnw
        casts = []
        for i, w_src in enumerate((fk_d, fr_d, fv_d)):
            dma("G", wscr[i].ap(), w_src.ap(), sem_stg[0])
            casts.append(dtok(sem_stg[0]))
        for i, w_dst in enumerate((wkT, wrT, wvT)):
            for cjc in range(CJ):
                emit("X", lambda x, i=i, cjc=cjc, w_dst=w_dst:
                     x.dma_start(out=w_dst[:, cjc * 1024:(cjc + 1) * 1024],
                                 in_=A_(wscr[i], cjc * 128,
                                        [[DM, DM], [1, 128]]),
                                 transpose=True),
                     waits=(casts[i],), dma_sem=sem_stg[1])
                key = sem_stg[1].name if hasattr(sem_stg[1], "name") else id(sem_stg[1])
                dma_count[key] = dma_count.get(key, 0) + 1
        t_ffnw = dtok(sem_stg[1])

    bank_free = {0: None, 1: None, 2: None, 3: None}
    ST = {
        "scan_prev": None, "prevcol": t_prev0, "xn2T_last": t_prev0,
        "z_done": [None] * NTC, "z_tiles": [None] * NTC,
        "store_all": [None] * (NTC + 1),
        "xT_free": [t_consts, t_consts],
        "vv_last": [None] * NTC, "kk_last": [None] * NTC,
        "smm": [None] * NTC, "rmm": [None] * NTC,
        "t_y0": [None] * NTC, "t_q": [None] * NTC, "t_p": [None] * NTC,
        "t_u": [None] * NTC, "t_evv": [None] * NTC,
        "t_xr": [None] * NTC, "t_xk": [None] * NTC, "t_diff": [None] * NTC,
        "t_xkl": [None] * NTC, "t_xrl": [None] * NTC, "t_d_ev": None,
        "t_x2l": [None] * NTC, "tka_last": [None] * NTC,
        "t_f": [None] * NTC,
        "r_done": [None] * NTC, "xbeta": [None] * NTC,
        "t_x2": [None] * NTC, "t_c2": [None] * NTC,
    }

    def stats_base(tc):
        par = (tc % 2) * 76
        return par + 0, par + 24, par + 48, par + 72

    def xtile(tc, ts):
        return xsb[:, (tc % 2) * 4096 + ts * 1024:
                   (tc % 2) * 4096 + (ts + 1) * 1024]

    def colmath_pair(base, ts0, SCR, eps, wa, wb, split_sum=False):
        def c(k, base=base, ts0=ts0):
            return stats[:, base + k + ts0:base + k + ts0 + 2]
        scr = stats[:, SCR + ts0:SCR + ts0 + 2]
        if split_sum:
            t0 = emit("V", lambda v, c=c: v.tensor_add(
                c(0), c(0), c(4)), waits=tuple(wa))
            wa = (t0,)
        t_a = emit("V", lambda v, c=c, scr=scr: v.tensor_mul(
            scr, c(0), c(0)), waits=tuple(wa))
        t3a = emit("V", lambda v, c=c: v.tensor_add(
            c(8), c(8), c(12)), waits=tuple(wb))
        t_b = emit("V", lambda v, c=c, eps=eps: v.tensor_scalar(
            out=c(8), in0=c(8), scalar1=INV_D, scalar2=eps,
            op0=ALU.mult, op1=ALU.add), waits=(t3a,))
        t_c = emit("V", lambda v, c=c, scr=scr: v.scalar_tensor_tensor(
            out=c(8), in0=scr, scalar=-INV_D * INV_D, in1=c(8),
            op0=ALU.mult, op1=ALU.add), waits=(t_a, t_b))
        t_d = emit("V", lambda v, c=c, scr=scr: v.reciprocal(scr, c(8)),
                   waits=(t_c,))
        t_e = emit("S", lambda s, c=c, scr=scr: s.activation(
            c(16), scr, AF.Sqrt), waits=(t_d,))
        t_f = emit("V", lambda v, c=c: v.scalar_tensor_tensor(
            out=c(20), in0=c(0), scalar=-INV_D, in1=c(16),
            op0=ALU.mult, op1=ALU.mult), waits=(t_e,))
        return t_f

    def ln_stats(tc, base, SCR, eps, src_tile, extra_red=(), extra_sq=()):
        t_cm = []
        for half in range(2):
            tred, tsq = [], []
            for ts in (2 * half, 2 * half + 1):
                tred.append(emit("V", lambda v, a=src_tile(ts),
                                 c=stats[:, base + ts:base + ts + 1]:
                                 v.reduce_sum(c, a, axis=AX.X),
                                 waits=tuple(extra_red)))
                for hh in range(2):
                    tsq.append(emit("S", lambda s, a=src_tile(ts), hh=hh,
                                    c=stats[:, base + 8 + 4 * hh + ts:
                                            base + 9 + 4 * hh + ts]:
                                    s.activation(
                                        sq[:, 0:512],
                                        bass.AP(a.tensor, a.offset + hh * 512,
                                                [[a.ap[0][0], 128], [1, 512]]),
                                        AF.Square, accum_out=c),
                                    waits=tuple(extra_sq)))
            t_cm.append(colmath_pair(base, 2 * half, SCR, eps,
                                     (tred[-1],), (tsq[-1],)))
        return t_cm

    def phase_front_a(tc):
        slot = tc % 2
        ST1, STB, STC, SCR = stats_base(tc)
        if tc >= 2:
            t_x[tc] = load_x(tc, waitl=ST["z_tiles"][tc - 2])

        t_ln1 = ln_stats(tc, ST1, SCR, EPS_PRE, lambda ts: xtile(tc, ts),
                         extra_red=(t_x[tc],), extra_sq=(t_x[tc],))
        t_c = []
        for ts in range(NTS):
            wl = [t_ln1[ts // 2]] if ts % 2 == 0 else []
            if ts == 0 and tc > 0:
                wl.append(ST["t_f"][tc - 1])
            t_c.append(emit("S", lambda s, a=xtile(tc, ts), ts=ts,
                            bi=stats[:, ST1 + 20 + ts:ST1 + 21 + ts],
                            sc=stats[:, ST1 + 16 + ts:ST1 + 17 + ts]:
                            s.activation(xnb[:, ts * 1024:(ts + 1) * 1024], a,
                                         AF.Identity, bias=bi, scale=sc),
                            waits=tuple(wl)))
        t_xbeta = emit("G", lambda g, xoff=slot * 4096: g.tensor_add(
            xsb[:, xoff:xoff + 4096], xsb[:, xoff:xoff + 4096],
            A_(bb, 0, [[DM, 128], [0, NTS], [1, DM]])),
            waits=(t_c[-1], t_ln1[-1], t_gb, t_x[tc]))
        ST["xbeta"][tc] = t_xbeta
        ST["t_c1"] = t_c

    def phase_front_b(tc):
        slot = tc % 2
        xTb = slot * 4096
        t_c = ST["t_c1"]
        t_d_ev = []
        for cj in range(CJ):
            rot = cj % 2
            ttr = None
            for ts in range(NTS):
                wl = [t_c[ts]]
                if ts == 0 and trbank_free[rot] is not None:
                    wl.append(trbank_free[rot])
                ttr = emit("T", lambda t, cj=cj, ts=ts, rot=rot: t.transpose(
                    p_tr[rot][:, ts * 128:(ts + 1) * 128],
                    xnb[:, ts * 1024 + cj * 128: ts * 1024 + (cj + 1) * 128],
                    eyeb[:]), waits=tuple(wl))
            tev = emit("V", lambda v, cj=cj, rot=rot, xTb=xTb: v.tensor_copy(
                xT[:, xTb + cj * 512: xTb + (cj + 1) * 512],
                p_tr[rot][:, 0:512]),
                waits=(ttr, ST["r_done"][tc - 2] if tc >= 2 else None))
            trbank_free[rot] = tev
            t_d_ev.append(tev)
        ST["t_d"] = t_d_ev[-1]
        ST["t_d_ev"] = t_d_ev

        qk_prev_reader = ST["t_y0"][tc - 1] if tc > 0 else None
        aug_prev = ST["t_q"][tc - 1] if tc > 0 else None
        for pi, (wsb, bcol, dstt, doff, twt) in enumerate(
                ((wq, bqc, qkbuf, 0, t_wq), (wk, bkc, xnb, 0, t_wkv),
                 (wv, bvc, vsb, 0, t_wkv))):
            for jc in range(CI):
                bk = 2 + (pi * CI + jc) % 2
                tmm = None
                for cj in range(CJ):
                    wl = [ST["t_d_ev"][cj]]
                    if cj == 0:
                        wl.append(twt)
                        if bank_free[bk] is not None:
                            wl.append(bank_free[bk])
                    tmm = emit("T", lambda t, wsb=wsb, jc=jc, cj=cj, bk=bk,
                               xTb=xTb: t.matmul(
                                   p_mm[bk][:, 0:512],
                                   wsb[:, cj * 512 + jc * 128:
                                       cj * 512 + (jc + 1) * 128],
                                   xT[:, xTb + cj * 512: xTb + (cj + 1) * 512],
                                   start=(cj == 0), stop=(cj == CJ - 1)),
                               waits=tuple(wl))
                wl = [tmm]
                if pi == 0 and jc == 0 and qk_prev_reader is not None:
                    wl.append(qk_prev_reader)
                if pi == 2 and jc == 0 and aug_prev is not None:
                    wl.append(aug_prev)
                if pi == 1 and jc == 0 and tc > 0:
                    wl.append(ST["tka_last"][tc - 1])
                tev = emit("V", lambda v, dstt=dstt, doff=doff, jc=jc,
                           bk=bk, bcol=bcol: v.tensor_scalar(
                               out=dstt[:, doff + jc * 512:
                                        doff + (jc + 1) * 512],
                               in0=p_mm[bk][:, 0:512],
                               scalar1=bcol[:, jc:jc + 1], scalar2=None,
                               op0=ALU.add), waits=tuple(wl))
                bank_free[bk] = tev
                if pi == 0:
                    ST["t_evq"] = tev
                elif pi == 1:
                    ST["t_evk"] = tev
                else:
                    ST["t_evv"][tc] = tev

    def phase_tail2(tc):
        nonlocal p_sm_free, p_st_free
        xTb = (tc % 2) * 4096
        if tc > 0:
            ST["prevcol"] = emit("V", lambda v: v.tensor_copy(
                A_(xn2T, 0, [[4104, 128], [513, CJ]]),
                A_(xn2T, 512, [[4104, 128], [513, CJ]])),
                waits=(ST["xn2T_last"],))
        wbank = [p_sm, p_st]
        wfree = [p_sm_free, p_st_free]
        t_w_ev = []
        for cj in range(CJ):
            rot = cj % 2
            ttr = None
            for ts in range(NTS):
                wl = [ST["t_c2"][tc][ts]]
                if ts == 0 and wfree[rot] is not None:
                    wl.append(wfree[rot])
                ttr = emit("T", lambda t, cj=cj, ts=ts, rot=rot: t.transpose(
                    wbank[rot][:, ts * 128:(ts + 1) * 128],
                    aA[:, ts * 1024 + cj * 128: ts * 1024 + (cj + 1) * 128],
                    eyef[:]), waits=tuple(wl))
            tev = emit("S", lambda s, cj=cj, rot=rot: s.activation(
                xn2T[:, cj * 513 + 1: cj * 513 + 513], wbank[rot][:, 0:512],
                AF.Identity), waits=(ttr, ST["prevcol"]))
            wfree[rot] = tev
            t_w_ev.append(tev)
        p_sm_free, p_st_free = wfree
        ST["xn2T_last"] = t_w_ev[-1]

        t_diff = emit("V", lambda v: v.tensor_sub(
            aA[:, 0:4096], A_(xn2T, 1, [[4104, 128], [513, CJ], [1, 512]]),
            A_(xn2T, 0, [[4104, 128], [513, CJ], [1, 512]])),
            waits=(t_w_ev[-1],))
        ST["t_diff"][tc] = t_diff
        t_xk = []
        for cj in range(CJ):
            wl = [t_diff, t_consts]
            if cj == 0:
                wl.append(ST["kk_last"][tc - 1] if tc > 0 else t_ffnw)
            t_xk.append(emit("V", lambda v, cj=cj: v.scalar_tensor_tensor(
                out=bigb[:, cj * 512:(cj + 1) * 512],
                in0=aA[:, cj * 512:(cj + 1) * 512],
                scalar=mukc[:, cj:cj + 1],
                in1=A_(xn2T, cj * 513, [[4104, 128], [1, 512]]),
                op0=ALU.mult, op1=ALU.add), waits=tuple(wl)))
        ST["t_xk"][tc] = t_xk[-1]
        ST["t_xkl"][tc] = t_xk
        t_xr = []
        for cj in range(CJ):
            wl = [t_diff, t_consts]
            if cj == 0:
                wl.append(ST["t_evv"][tc])
            t_xr.append(emit("V", lambda v, cj=cj, xTb=xTb:
                        v.scalar_tensor_tensor(
                            out=xT[:, xTb + cj * 512: xTb + (cj + 1) * 512],
                            in0=aA[:, cj * 512:(cj + 1) * 512],
                            scalar=murc[:, cj:cj + 1],
                            in1=A_(xn2T, cj * 513, [[4104, 128], [1, 512]]),
                            op0=ALU.mult, op1=ALU.add), waits=tuple(wl)))
        ST["t_xr"][tc] = t_xr[-1]
        ST["t_xrl"][tc] = t_xr

    def phase_mid_a1(tc):
        nonlocal p_sm_free, p_st_free
        vvp = ST["vv_last"][tc - 2] if tc >= 2 else None

        t_f = emit("V", lambda v: v.tensor_mul(
            bigb[:, 4096:6144], xnb[:, 0:2048], qkbuf[:, 0:2048]),
            waits=(ST["t_evq"], ST["t_evk"], vvp))
        ST["t_f"][tc] = t_f

        tmm = None
        for ts in range(NTS):
            for ci in range(CI):
                wl = []
                if ts == 0 and ci == 0:
                    wl = [t_f, t_consts]
                    if p_st_free is not None:
                        wl.append(p_st_free)
                tmm = emit("T", lambda t, ts=ts, ci=ci: t.matmul(
                    p_st[:, ts * 8:(ts + 1) * 8],
                    bigb[:, 4096 + ci * 512 + ts * 128:
                         4096 + ci * 512 + (ts + 1) * 128],
                    sels[:, ci * 8:(ci + 1) * 8],
                    start=(ci == 0), stop=(ci == CI - 1)), waits=tuple(wl))
        ST["smm"][tc] = tmm
        t_j = emit("V", lambda v: v.reduce_max(
            mx4[:], A_(p_st, 0, [[512, 128], [8, 4], [1, 8]]), axis=AX.X,
            negate=True), waits=(tmm,))
        ST["t_jm"] = t_j

    def phase_mid_a2(tc):
        nonlocal p_sm_free, p_st_free
        texp = None
        for g in range(4):
            texp = emit("S", lambda s, g=g: s.activation(
                stm[:, g * 8:(g + 1) * 8], p_st[:, g * 8:(g + 1) * 8],
                AF.Exp, bias=mx4[:, g:g + 1]),
                waits=(ST["t_jm"],) if g == 0 else ())
        p_st_free = texp
        t_j = emit("V", lambda v: v.reduce_sum(
            rc4[:], A_(stm, 0, [[32, 128], [8, 4], [1, 8]]), axis=AX.X),
            waits=(texp,))
        t_j = emit("V", lambda v: v.reciprocal(rc4[:], rc4[:]), waits=(t_j,))
        t_j = emit("V", lambda v: v.tensor_mul(
            A_(stm, 0, [[32, 128], [8, 4], [1, 8]]),
            A_(stm, 0, [[32, 128], [8, 4], [1, 8]]),
            A_(rc4, 0, [[4, 128], [1, 4], [0, 8]])), waits=(t_j,))
        ttr = None
        for ts in range(NTS):
            wl = []
            if ts == 0:
                wl = [t_j]
                if p_sm_free is not None:
                    wl.append(p_sm_free)
            ttr = emit("T", lambda t, ts=ts: t.transpose(
                p_sm[0:8, ts * 128:(ts + 1) * 128], stm[:, ts * 8:(ts + 1) * 8],
                eyef[:]), waits=tuple(wl))
        p_sm_free = emit("V", lambda v: v.tensor_copy(arow[:],
                                                      p_sm[0:8, 0:512]),
                         waits=(ttr,))
        t_l = p_sm_free

        t_u = []
        for ci in range(CI):
            bk = 2 + ci % 2
            wl = [t_l, t_consts]
            if bank_free[bk] is not None:
                wl.append(bank_free[bk])
            tmm = emit("T", lambda t, ci=ci, bk=bk: t.matmul(
                p_mm[bk][:, 0:512], selb[0:8, ci * 128:(ci + 1) * 128],
                arow[0:8, 0:512], start=True, stop=True), waits=tuple(wl))
            wl = [tmm, ST["t_evv"][tc], t_decay]
            if ci == 0 and tc > 0:
                wl.append(ST["t_q"][tc - 1])
            tn = emit("V", lambda v, ci=ci, bk=bk: v.scalar_tensor_tensor(
                out=aB[:, ci * 512:(ci + 1) * 512], in0=p_mm[bk][:, 0:512],
                scalar=omd[:, ci:ci + 1], in1=vsb[:, ci * 512:(ci + 1) * 512],
                op0=ALU.mult, op1=ALU.mult), waits=tuple(wl))
            bank_free[bk] = tn
            t_u.append(tn)
        ST["t_u"][tc] = t_u[-1]

        t_scan = []
        for ci in range(CI):
            wl = [t_u[ci]]
            if tc == 0 and ci == 0:
                wl.append(t_carry0)
            t_scan.append(emit("V", lambda v, ci=ci: v.tensor_tensor_scan(
                out=aB[:, ci * 512:(ci + 1) * 512],
                data0=A_(decay, ci, [[4, 128], [0, 512]]),
                data1=aB[:, ci * 512:(ci + 1) * 512],
                initial=carry[:, ci:ci + 1], op0=ALU.mult, op1=ALU.add),
                waits=tuple(wl)))
        ST["scan_prev"] = emit("V", lambda v: v.tensor_copy(
            carry[:, 0:4], A_(aB, 511, [[2048, 128], [512, 4]])),
            waits=(t_scan[-1],))

        t_p = []
        for ci in range(CI):
            t_p.append(emit("S", lambda s, ci=ci: s.activation(
                vsb[:, ci * 512:(ci + 1) * 512],
                qkbuf[:, ci * 512:(ci + 1) * 512],
                AF.Tanh, scale=augph[:, ci:ci + 1]),
                waits=(t_augph, t_u[ci])))
        ST["t_p"][tc] = t_p[-1]
        t_y0 = emit("V", lambda v: v.tensor_mul(
            aB[:, 0:2048], aB[:, 0:2048], qkbuf[:, 0:2048]),
            waits=(ST["scan_prev"],))
        ST["t_y0"][tc] = t_y0
    def phase_mid_b(tc):
        t_q = emit("V", lambda v: v.scalar_tensor_tensor(
            out=bigb[:, 6144:8192], in0=vsb[:, 0:2048], scalar=0.5,
            in1=aB[:, 0:2048], op0=ALU.mult, op1=ALU.add),
            waits=(ST["t_p"][tc], ST["t_y0"][tc],
                   ST["vv_last"][tc - 1] if tc > 0 else None))
        ST["t_q"][tc] = t_q

    def phase_mid_r(tc):
        ST1, STB, STC, SCR = stats_base(tc)
        t_q = ST["t_q"][tc]
        t_oev = []
        for ts in range(NTS):
            for jh in range(2):
                bk = (ts * 2 + jh) % 4
                tmm = None
                for ic in range(CI):
                    wl = []
                    if ic == 0:
                        wl = [t_q, t_bff]
                        if bank_free[bk] is not None:
                            wl.append(bank_free[bk])
                    tmm = emit("T", lambda t, ts=ts, jh=jh, ic=ic, bk=bk:
                               t.matmul(p_mm[bk][:, 0:512],
                                        bigb[:, 6144 + ic * 512 + ts * 128:
                                             6144 + ic * 512 + (ts + 1) * 128],
                                        wo[:, ic * 1024 + jh * 512:
                                           ic * 1024 + (jh + 1) * 512],
                                        start=(ic == 0), stop=False),
                               waits=tuple(wl))
                tmm = emit("T", lambda t, jh=jh, bk=bk: t.matmul(
                    p_mm[bk][:, 0:512], ones1b[0:1, 0:128],
                    bffr[0:1, jh * 512:(jh + 1) * 512], start=False, stop=True))
                wl = [tmm]
                if ts == 0 and jh == 0 and ST["store_all"][tc - 1] is not None:
                    wl.append(ST["store_all"][tc - 1])
                tev = emit("S", lambda s, ts=ts, jh=jh, bk=bk,
                           c=stats[:, STB + 4 * jh + ts:STB + 1 + 4 * jh + ts]:
                           s.activation(
                               aA[:, ts * 1024 + jh * 512:
                                  ts * 1024 + (jh + 1) * 512],
                               p_mm[bk][:, 0:512], AF.Identity, accum_out=c),
                           waits=tuple(wl))
                bank_free[bk] = tev
                t_oev.append(tev)
        ST["rmm"][tc] = tmm
        ST["t_oev"] = t_oev

    def phase_tail1_s1(tc, half):
        ST1, STB, STC, SCR = stats_base(tc)
        t_oev = ST["t_oev"]
        if half == 0:
            ST["t_c2"][tc] = [None] * NTS
            ST["t_x2l"][tc] = [None] * NTS
            ST["t_t1"] = [None] * NTS
        p0 = 2 * half
        tsq = []
        for ts in (p0, p0 + 1):
            for hh in range(2):
                tsq.append(emit("S", lambda s, ts=ts, hh=hh,
                                c=stats[:, STB + 8 + 4 * hh + ts:
                                        STB + 9 + 4 * hh + ts]:
                                s.activation(
                                    sq[:, 0:512],
                                    aA[:, ts * 1024 + hh * 512:
                                       ts * 1024 + (hh + 1) * 512],
                                    AF.Square, accum_out=c),
                                waits=(t_oev[2 * ts + hh],)))
        t_lnb = colmath_pair(STB, p0, SCR, EPS_ATTN,
                             (t_oev[2 * p0 + 3],), (tsq[-1],),
                             split_sum=True)
        for ts in (p0, p0 + 1):
            ST["t_t1"][ts] = emit("S", lambda s, ts=ts,
                                  bi=stats[:, STB + 20 + ts:STB + 21 + ts],
                                  sc=stats[:, STB + 16 + ts:STB + 17 + ts]:
                                  s.activation(
                                      aA[:, ts * 1024:(ts + 1) * 1024],
                                      aA[:, ts * 1024:(ts + 1) * 1024],
                                      AF.Identity, bias=bi, scale=sc),
                                  waits=(t_lnb,) if ts == p0 else ())

    def phase_tail1_s2(tc, half):
        ST1, STB, STC, SCR = stats_base(tc)
        t_c2 = ST["t_c2"][tc]
        x2_all = ST["t_x2l"][tc]
        t_x2 = None
        p0 = 2 * half
        x2l = {}
        for ts in (p0, p0 + 1):
            tg = emit("V", lambda v, ts=ts: v.tensor_mul(
                aA[:, ts * 1024:(ts + 1) * 1024],
                aA[:, ts * 1024:(ts + 1) * 1024], gb[:, 0:DM]),
                waits=(ST["t_t1"][ts], t_gb))
            t_x2 = emit("V", lambda v, a=xtile(tc, ts), ts=ts,
                        c=stats[:, STC + ts:STC + ts + 1]:
                        v.scalar_tensor_tensor(
                            out=a, in0=aA[:, ts * 1024:(ts + 1) * 1024],
                            scalar=1.0, in1=a, op0=ALU.mult, op1=ALU.add,
                            accum_out=c),
                        waits=(tg, ST["xbeta"][tc]))
            x2l[ts] = t_x2
            x2_all[ts] = t_x2
        tsq = []
        for ts in (p0, p0 + 1):
            for hh in range(2):
                tsq.append(emit("S", lambda s, ts=ts, hh=hh,
                                a=xtile(tc, ts),
                                c=stats[:, STC + 8 + 4 * hh + ts:
                                        STC + 9 + 4 * hh + ts]:
                                s.activation(
                                    sq[:, 0:512],
                                    bass.AP(a.tensor, a.offset + hh * 512,
                                            [[a.ap[0][0], 128], [1, 512]]),
                                    AF.Square, accum_out=c),
                                waits=(x2l[ts],)))
        t_lnc = colmath_pair(STC, p0, SCR, EPS_PRE,
                             (t_x2,), (tsq[-1],))
        for ts in (p0, p0 + 1):
            t_c2[ts] = emit("S", lambda s, a=xtile(tc, ts), ts=ts,
                            bi=stats[:, STC + 20 + ts:STC + 21 + ts],
                            sc=stats[:, STC + 16 + ts:STC + 17 + ts]:
                            s.activation(
                                aA[:, ts * 1024:(ts + 1) * 1024], a,
                                AF.Identity, bias=bi, scale=sc),
                            waits=(t_lnc,) if ts == p0 else ())
        ST["t_x2"][tc] = t_x2

    def phase_ffn(tc):
        xTb = (tc % 2) * 4096
        t_ka = []
        trl_last = None
        for dj in range(CJ):
            bk = dj % 2
            tmm = None
            for cjc in range(CJ):
                wl = [ST["t_xkl"][tc][cjc]]
                if cjc == 0:
                    wl.append(t_ffnw)
                    if bank_free[bk] is not None:
                        wl.append(bank_free[bk])
                tmm = emit("T", lambda t, dj=dj, cjc=cjc, bk=bk: t.matmul(
                    p_mm[bk][:, 0:512],
                    wkT[:, cjc * 1024 + dj * 128: cjc * 1024 + (dj + 1) * 128],
                    bigb[:, cjc * 512:(cjc + 1) * 512],
                    start=(cjc == 0), stop=(cjc == CJ - 1)), waits=tuple(wl))
            trl = emit("V", lambda v, bk=bk, dj=dj: v.tensor_scalar_max(
                xnb[:, (dj % 2) * 512:(dj % 2 + 1) * 512], p_mm[bk][:, 0:512],
                0.0), waits=(tmm, ST["t_f"][min(tc + 1, NTC - 1)]))
            trl_last = trl
            wl = [trl]
            if dj < 4:
                wl.append(ST["smm"][min(tc + 1, NTC - 1)])
            else:
                wl.append(ST["rmm"][tc])
            tka = emit("S", lambda s, dj=dj: s.activation(
                bigb[:, 4096 + dj * 512: 4096 + (dj + 1) * 512],
                xnb[:, (dj % 2) * 512:(dj % 2 + 1) * 512], AF.Square),
                waits=tuple(wl))
            bank_free[bk] = tka
            t_ka.append(tka)
        kk_mm_last = tmm
        ST["tka_last"][tc] = t_ka[-1]

        ST["kk_last"][tc] = kk_mm_last

        last_tz = None
        tz3 = None
        z_tiles = []
        t_sg = []
        for ts in range(NTS):
            t_tr = []
            for dh in range(2):
                bk = 2 + (ts * 2 + dh) % 2
                tmm = None
                for cjc in range(CJ):
                    wl = [ST["t_xrl"][tc][cjc]]
                    if cjc == 0:
                        wl.append(t_ffnw)
                        if bank_free[bk] is not None:
                            wl.append(bank_free[bk])
                    tmm = emit("T", lambda t, ts=ts, dh=dh, cjc=cjc, bk=bk,
                               xTb=xTb: t.matmul(
                                   p_mm[bk][:, 0:512],
                                   xT[:, xTb + cjc * 512 + ts * 128:
                                      xTb + cjc * 512 + (ts + 1) * 128],
                                   wrT[:, cjc * 1024 + dh * 512:
                                       cjc * 1024 + (dh + 1) * 512],
                                   start=(cjc == 0), stop=(cjc == CJ - 1)),
                               waits=tuple(wl))
                wl = [tmm, ST["t_xk"][tc], ST["t_diff"][tc],
                      ST["t_c2"][tc][-1]]
                ttr2 = emit("S", lambda s, ts=ts, dh=dh, bk=bk: s.activation(
                    aA[:, ts * 1024 + dh * 512: ts * 1024 + (dh + 1) * 512],
                    p_mm[bk][:, 0:512], AF.Tanh, scale=0.5), waits=tuple(wl))
                bank_free[bk] = ttr2
                t_tr.append(ttr2)
            sg = emit("S", lambda s, ts=ts: s.activation(
                aA[:, ts * 1024:(ts + 1) * 1024],
                aA[:, ts * 1024:(ts + 1) * 1024],
                AF.Identity, bias=halfc[:, 0:1], scale=halfc[:, 0:1]),
                waits=(t_tr[1], t_half))
            t_sg.append(sg)
            for ih in range(2):
                bk = (ts * 2 + ih) % 2
                tmm = None
                for dj in range(CJ):
                    wl = []
                    if dj == 0:
                        wl = [t_ka[-1], t_ffnw]
                        if bank_free[bk] is not None:
                            wl.append(bank_free[bk])
                    tmm = emit("T", lambda t, ts=ts, ih=ih, dj=dj, bk=bk:
                               t.matmul(
                                   p_mm[bk][:, 0:512],
                                   bigb[:, 4096 + dj * 512 + ts * 128:
                                        4096 + dj * 512 + (ts + 1) * 128],
                                   wvT[:, dj * 1024 + ih * 512:
                                       dj * 1024 + (ih + 1) * 512],
                                   start=(dj == 0), stop=(dj == CJ - 1)),
                               waits=tuple(wl))
                ST["vv_last"][tc] = tmm
                tz = emit("V", lambda v, ts=ts, ih=ih, bk=bk: v.tensor_mul(
                    aA[:, ts * 1024 + ih * 512: ts * 1024 + (ih + 1) * 512],
                    p_mm[bk][:, 0:512],
                    aA[:, ts * 1024 + ih * 512: ts * 1024 + (ih + 1) * 512]),
                    waits=(tmm, sg))
                bank_free[bk] = tz
                last_tz = tz
            tz3 = emit("V", lambda e, ts=ts, a=xtile(tc, ts): e.tensor_add(
                aA[:, ts * 1024:(ts + 1) * 1024],
                aA[:, ts * 1024:(ts + 1) * 1024], a),
                waits=(last_tz, ST["t_x2l"][tc][ts]))
            z_tiles.append(tz3)
            out_ap = A_(y_d, (tc * TCW + ts * 128) * DM, [[DM, 128], [1, DM]])
            dma("X", out_ap, aA[:, ts * 1024:(ts + 1) * 1024], sem_do,
                waits=(tz3,))
        ST["r_done"][tc] = t_sg[-1]
        ST["z_done"][tc] = tz3
        ST["z_tiles"][tc] = z_tiles
        ST["store_all"][tc] = dtok(sem_do)

    phase_front_a(0)
    emit_ffn_weights()
    phase_front_b(0)
    phase_mid_a1(0)
    phase_mid_a2(0)
    for tc in range(NTC):
        if tc > 0:
            phase_tail2(tc - 1)
            phase_ffn(tc - 1)
        phase_mid_b(tc)
        phase_mid_r(tc)
        if tc + 1 < NTC:
            phase_front_a(tc + 1)
            phase_front_b(tc + 1)
        phase_tail1_s1(tc, 0)
        phase_tail1_s1(tc, 1)
        if tc + 1 < NTC:
            phase_mid_a1(tc + 1)
        phase_tail1_s2(tc, 0)
        phase_tail1_s2(tc, 1)
        if tc + 1 < NTC:
            phase_mid_a2(tc + 1)
    phase_tail2(NTC - 1)
    phase_ffn(NTC - 1)

    emit("X", None, waits=(dtok(sem_do),))

    marks = {e: set() for e in ENGS}
    for e in ENGS:
        for ent in sched[e]:
            for w in ent["waits"]:
                if w[0] == "E":
                    marks[w[1]].add(w[2])
    mile = {}
    for e in ENGS:
        mile[e] = {idx: n + 1 for n, idx in enumerate(sorted(marks[e]))}

    def runner(e):
        def run(eng):
            waited = {}
            for idx, ent in enumerate(sched[e]):
                for w in ent["waits"]:
                    if w[0] == "E":
                        _, pe, pidx = w
                        val = mile[pe][pidx]
                        key = "E" + pe
                        if waited.get(key, 0) < val:
                            eng.wait_ge(esem[pe], val)
                            waited[key] = val
                    else:
                        _, dsem, val = w
                        key = dsem.name if hasattr(dsem, "name") else id(dsem)
                        if waited.get(("D", key), 0) < val:
                            eng.wait_ge(dsem, val)
                            waited[("D", key)] = val
                if ent["fn"] is None:
                    continue
                inst = ent["fn"](eng)
                if ent["dma_sem"] is not None:
                    inst = inst.then_inc(ent["dma_sem"], 16)
                if idx in mile[e]:
                    inst.then_inc(esem[e], 1)
        return run

    with es:
        with nc.Block() as blk:
            blk.tensor(runner("T"))
            blk.vector(runner("V"))
            blk.scalar(runner("S"))
            blk.gpsimd(runner("G"))
            blk.sync(runner("X"))
    return nc


_NC_CACHE = None


def kernel(L0_attn_x, L0_attn_Wq, L0_attn_bq, L0_attn_Wk, L0_attn_bk,
           L0_attn_Wv, L0_attn_bv, L0_attn_Wo, L0_attn_bo, L0_attn_alpha,
           L0_attn_aug_param, L0_attn_ln_scale, L0_attn_ln_bias,
           L0_ffn_W_r, L0_ffn_W_k, L0_ffn_W_v, L0_ffn_mu_r, L0_ffn_mu_k):
    global _NC_CACHE
    if _NC_CACHE is None:
        _NC_CACHE = build_nc()
    nc = _NC_CACHE

    c = np.ascontiguousarray
    f32 = np.float32

    def cols(vec, k):
        return c(np.asarray(vec, f32).reshape(k, 128).T)

    base = {
        "wq": c(np.asarray(L0_attn_Wq, f32)),
        "wk": c(np.asarray(L0_attn_Wk, f32)),
        "wv": c(np.asarray(L0_attn_Wv, f32)),
        "wo": c(np.asarray(L0_attn_Wo, f32)),
        "fr": c(np.asarray(L0_ffn_W_r, f32)),
        "fk": c(np.asarray(L0_ffn_W_k, f32)),
        "fv": c(np.asarray(L0_ffn_W_v, f32)),
        "bqc": cols(L0_attn_bq, 4),
        "bkc": cols(L0_attn_bk, 4),
        "bvc": cols(L0_attn_bv, 4),
        "alc": cols(np.asarray(L0_attn_alpha, f32).reshape(512), 4),
        "apc": cols(np.asarray(L0_attn_aug_param, f32).reshape(512), 4),
        "murc": cols(L0_ffn_mu_r, 8),
        "mukc": cols(L0_ffn_mu_k, 8),
        "grow": c(np.asarray(L0_attn_ln_scale, f32).reshape(1, DM)),
        "brow": c(np.asarray(L0_attn_ln_bias, f32).reshape(1, DM)),
        "borow": c(np.asarray(L0_attn_bo, f32).reshape(1, DM)),
        "eyeb": np.eye(128, dtype=BFnp),
        "eyef": np.eye(128, dtype=f32),
        "ones1b": np.ones((1, 128), BFnp),
        "ones1f": np.ones((1, 128), f32),
        "onescb": np.ones((128, 1), BFnp),
    }
    sels = np.zeros((128, 32), f32)
    selb = np.zeros((8, 512), f32)
    for ci in range(4):
        for p in range(128):
            h = 2 * ci + (1 if p >= 64 else 0)
            sels[p, ci * 8 + h] = 0.125
            selb[h, ci * 128 + p] = 1.0
    base["sels"] = sels.astype(BFnp)
    base["selb"] = selb

    xs = np.asarray(L0_attn_x, f32)
    in_maps = []
    for b in range(8):
        m = dict(base)
        m["x"] = c(xs[b])
        in_maps.append(m)

    res = run_bass_kernel_spmd(nc, in_maps, core_ids=list(range(8)))
    out = np.stack([np.asarray(r["y"]) for r in res.results], axis=0)
    return out.astype(np.float32)


# version 18
# speedup vs baseline: 1.0641x; 1.0134x over previous
import contextlib

import numpy as np
import ml_dtypes

import concourse.bass as bass
import concourse.mybir as mybir
from concourse.bass_utils import run_bass_kernel_spmd

F32 = mybir.dt.float32
BF16 = mybir.dt.bfloat16
AF = mybir.ActivationFunctionType
ALU = mybir.AluOpType
AX = mybir.AxisListType
BFnp = ml_dtypes.bfloat16

SEQ, DM, INNER = 2048, 1024, 512
NTC, TCW = 4, 512
NTS = 4
CJ = 8
CI = 4
EPS_PRE = 1e-5
EPS_ATTN = 1e-6
INV_D = 1.0 / DM


def build_nc():
    nc = bass.Bass("TRN2", target_bir_lowering=False,
                   dynamic_dma_scratch_size=8192,
                   detect_race_conditions=False)

    x_d = nc.dram_tensor("x", [SEQ, DM], F32, kind="ExternalInput")
    wq_d = nc.dram_tensor("wq", [DM, INNER], F32, kind="ExternalInput")
    wk_d = nc.dram_tensor("wk", [DM, INNER], F32, kind="ExternalInput")
    wv_d = nc.dram_tensor("wv", [DM, INNER], F32, kind="ExternalInput")
    wo_d = nc.dram_tensor("wo", [INNER, DM], F32, kind="ExternalInput")
    fr_d = nc.dram_tensor("fr", [DM, DM], F32, kind="ExternalInput")
    fk_d = nc.dram_tensor("fk", [DM, DM], F32, kind="ExternalInput")
    fv_d = nc.dram_tensor("fv", [DM, DM], F32, kind="ExternalInput")
    bqc_d = nc.dram_tensor("bqc", [128, 4], F32, kind="ExternalInput")
    bkc_d = nc.dram_tensor("bkc", [128, 4], F32, kind="ExternalInput")
    bvc_d = nc.dram_tensor("bvc", [128, 4], F32, kind="ExternalInput")
    alc_d = nc.dram_tensor("alc", [128, 4], F32, kind="ExternalInput")
    apc_d = nc.dram_tensor("apc", [128, 4], F32, kind="ExternalInput")
    murc_d = nc.dram_tensor("murc", [128, 8], F32, kind="ExternalInput")
    mukc_d = nc.dram_tensor("mukc", [128, 8], F32, kind="ExternalInput")
    grow_d = nc.dram_tensor("grow", [1, DM], F32, kind="ExternalInput")
    brow_d = nc.dram_tensor("brow", [1, DM], F32, kind="ExternalInput")
    borow_d = nc.dram_tensor("borow", [1, DM], F32, kind="ExternalInput")
    eyeb_d = nc.dram_tensor("eyeb", [128, 128], BF16, kind="ExternalInput")
    eyef_d = nc.dram_tensor("eyef", [128, 128], F32, kind="ExternalInput")
    sels_d = nc.dram_tensor("sels", [128, 32], BF16, kind="ExternalInput")
    selb_d = nc.dram_tensor("selb", [8, 512], F32, kind="ExternalInput")
    ones1b_d = nc.dram_tensor("ones1b", [1, 128], BF16, kind="ExternalInput")
    ones1f_d = nc.dram_tensor("ones1f", [1, 128], F32, kind="ExternalInput")
    onescb_d = nc.dram_tensor("onescb", [128, 1], BF16, kind="ExternalInput")
    y_d = nc.dram_tensor("y", [SEQ, DM], F32, kind="ExternalOutput")
    wscr = [nc.dram_tensor(f"wscr{i}", [DM, DM], BF16, kind="Internal")
            for i in range(3)]

    es = contextlib.ExitStack()

    def sb(name, shape, dtype):
        return es.enter_context(nc.sbuf_tensor(name, shape, dtype))

    def ps(name, shape, dtype):
        return es.enter_context(nc.psum_tensor(name, shape, dtype))

    def sem(name):
        return es.enter_context(nc.semaphore(name))

    wq = sb("wq_sb", [128, 4096], BF16)
    wk = sb("wk_sb", [128, 4096], BF16)
    wv = sb("wv_sb", [128, 4096], BF16)
    wo = sb("wo_sb", [128, 4096], BF16)
    wrT = sb("wrT", [128, 8192], BF16)
    wkT = sb("wkT", [128, 8192], BF16)
    wvT = sb("wvT", [128, 8192], BF16)
    xsb = sb("xsb", [128, 8192], F32)
    xnb = sb("xnb", [128, 4096], BF16)
    xT = sb("xT", [128, 8192], BF16)
    bigb = sb("bigb", [128, 8192], BF16)
    qkbuf = sb("qkbuf", [128, 2048], BF16)
    aA = sb("aA", [128, 4096], F32)
    aB = sb("aB", [128, 2048], F32)
    vsb = sb("vsb", [128, 2048], F32)
    xn2T = sb("xn2T", [128, 4104], BF16)
    gb = sb("gb", [128, DM], F32)
    bb = sb("bb", [128, DM], F32)
    sq = sb("sq", [128, 512], F32)
    stats = sb("stats", [128, 152], F32)
    carry = sb("carry", [128, 4], F32)
    decay = sb("decay", [128, 4], F32)
    omd = sb("omd", [128, 4], F32)
    augph = sb("augph", [128, 4], F32)
    s8 = sb("s8", [8, 512], F32)
    arow = s8
    selb = sb("selb_sb", [8, 512], F32)
    stm = sb("stm", [128, 32], F32)
    atm = stm
    mx4 = sb("mx4", [128, 4], F32)
    rc4 = sb("rc4", [128, 4], F32)
    halfc = sb("halfc", [128, 1], F32)
    bffr = sb("bffr", [1, DM], BF16)
    eyeb = sb("eyeb_sb", [128, 128], BF16)
    eyef = sb("eyef_sb", [128, 128], F32)
    sels = sb("sels_sb", [128, 32], BF16)
    ones1b = sb("ones1b_sb", [1, 128], BF16)
    ones1f = sb("ones1f_sb", [1, 128], F32)
    onescb = sb("onescb_sb", [128, 1], BF16)
    bqc = sb("bqc_sb", [128, 4], F32)
    bkc = sb("bkc_sb", [128, 4], F32)
    bvc = sb("bvc_sb", [128, 4], F32)
    alc = sb("alc_sb", [128, 4], F32)
    apc = sb("apc_sb", [128, 4], F32)
    murc = sb("murc_sb", [128, 8], F32)
    mukc = sb("mukc_sb", [128, 8], F32)

    p_mm = [ps("p_mm0", [128, 512], F32), ps("p_mm1", [128, 512], F32),
            ps("p_mm2", [128, 512], F32), ps("p_mm3", [128, 512], F32)]
    p_tr = [ps("p_tr0", [128, 512], BF16), ps("p_tr1", [128, 512], BF16)]
    p_sm = ps("p_sm", [128, 512], F32)
    p_st = ps("p_st", [128, 512], F32)

    ENGS = ("T", "V", "S", "G", "X")
    esem = {e: sem("sem_" + e) for e in ENGS}
    sem_dc = sem("sem_dc")
    sem_dx = [sem("sem_dx0"), sem("sem_dx1")]
    sem_do = sem("sem_do")
    sem_wq = sem("sem_wq")
    sem_wkv = sem("sem_wkv")
    sem_wo = sem("sem_wo")
    sem_stg = [sem("sem_stg0"), sem("sem_stg1")]

    sched = {e: [] for e in ENGS}
    dma_count = {}

    def emit(e, fn, waits=(), dma_sem=None):
        idx = len(sched[e])
        wl = []
        for w in waits:
            if w is None:
                continue
            if w[0] == "D":
                wl.append(w)
            else:
                pe, pidx = w
                assert pidx < len(sched[pe]) or (pe == e and pidx < idx), \
                    "wait on future entry"
                if pe == e and idx - pidx >= 16:
                    continue
                wl.append(("E", pe, pidx))
        sched[e].append({"fn": fn, "waits": wl, "dma_sem": dma_sem})
        return (e, idx)

    def dma(e, out_ap, in_ap, dsem, waits=()):
        key = dsem.name if hasattr(dsem, "name") else id(dsem)
        dma_count[key] = dma_count.get(key, 0) + 1
        emit(e, lambda eng, o=out_ap, i=in_ap: eng.dma_start(out=o, in_=i),
             waits=waits, dma_sem=dsem)

    def dtok(dsem):
        key = dsem.name if hasattr(dsem, "name") else id(dsem)
        return ("D", dsem, 16 * dma_count.get(key, 0))

    A_ = bass.AP

    for dst, src in ((eyeb, eyeb_d), (eyef, eyef_d), (sels, sels_d),
                     (ones1b, ones1b_d), (ones1f, ones1f_d),
                     (onescb, onescb_d), (bqc, bqc_d), (bkc, bkc_d),
                     (bvc, bvc_d), (alc, alc_d), (apc, apc_d),
                     (murc, murc_d), (mukc, mukc_d)):
        dma("X", dst[:], src.ap(), sem_dc)
    dma("X", selb[:], selb_d.ap(), sem_dc)
    dma("X", vsb[0:1, 1024:2048], grow_d.ap(), sem_dc)
    dma("X", aB[0:1, 0:1024], brow_d.ap(), sem_dc)
    dma("X", aB[0:1, 1024:2048], borow_d.ap(), sem_dc)
    t_consts = dtok(sem_dc)

    def load_x(tc, waitl=None):
        slot = tc % 2
        for ts in range(NTS):
            in_ap = A_(x_d, (tc * TCW + ts * 128) * DM, [[DM, 128], [1, DM]])
            dma("X", xsb[:, slot * 4096 + ts * 1024:
                         slot * 4096 + (ts + 1) * 1024],
                in_ap, sem_dx[slot],
                waits=(waitl[ts],) if waitl is not None else ())
        return dtok(sem_dx[slot])

    t_x = [None] * NTC
    t_x[0] = load_x(0)
    t_x[1] = load_x(1)

    def cast_w_attn(dst, src, dsem):
        in_ap = A_(src, 0, [[INNER, 128], [128 * INNER, CJ], [1, INNER]])
        out_ap = A_(dst, 0, [[4096, 128], [512, CJ], [1, 512]])
        dma("G", out_ap, in_ap, dsem)

    cast_w_attn(wq, wq_d, sem_wq)
    t_wq = dtok(sem_wq)
    cast_w_attn(wk, wk_d, sem_wkv)
    cast_w_attn(wv, wv_d, sem_wkv)
    t_wkv = dtok(sem_wkv)
    dma("G", A_(wo, 0, [[4096, 128], [1024, CI], [1, DM]]),
        A_(wo_d, 0, [[DM, 128], [128 * DM, CI], [1, DM]]), sem_wo)
    t_wo = dtok(sem_wo)

    t_gb = None
    for half in range(2):
        for rowt, roff, dest in ((vsb, 1024, gb), (aB, 0, bb)):
            tmm = emit("T", lambda t, rowt=rowt, roff=roff, h=half: t.matmul(
                p_st[:, 0:512], ones1f[0:1, 0:128],
                rowt[0:1, roff + h * 512: roff + (h + 1) * 512],
                start=True, stop=True), waits=(t_consts, t_gb))
            t_gb = emit("V", lambda v, d=dest, h=half: v.tensor_copy(
                d[:, h * 512:(h + 1) * 512], p_st[:, 0:512]), waits=(tmm,))
    p_st_free = t_gb

    t_bff = None
    for jh in range(2):
        tmm = None
        for ic in range(CI):
            tmm = emit("T", lambda t, ic=ic, jh=jh: t.matmul(
                p_sm[0:1, 0:512], onescb[:, 0:1],
                wo[:, ic * 1024 + jh * 512: ic * 1024 + (jh + 1) * 512],
                start=(ic == 0), stop=(ic == CI - 1)),
                waits=(t_wo, t_consts, t_bff) if ic == 0 else ())
        t_bff = emit("V", lambda v, jh=jh: v.scalar_tensor_tensor(
            out=bffr[0:1, jh * 512:(jh + 1) * 512], in0=p_sm[0:1, 0:512],
            scalar=0.5, in1=aB[0:1, 1024 + jh * 512: 1024 + (jh + 1) * 512],
            op0=ALU.mult, op1=ALU.add), waits=(tmm, t_consts))
    p_sm_free = t_bff

    tdk = emit("S", lambda s: s.activation(stats[:, 72:76], alc[:], AF.Tanh,
                                           scale=0.5), waits=(t_consts,))
    tdk = emit("V", lambda v: v.tensor_scalar(out=decay[:], in0=stats[:, 72:76],
               scalar1=0.5, scalar2=0.5, op0=ALU.mult, op1=ALU.add),
               waits=(tdk,))
    t_decay = emit("V", lambda v: v.tensor_scalar(out=omd[:], in0=stats[:, 72:76],
                   scalar1=-0.5, scalar2=0.5, op0=ALU.mult, op1=ALU.add),
                   waits=(tdk,))
    t_augph = emit("V", lambda v: v.tensor_scalar(out=augph[:], in0=apc[:],
                   scalar1=0.5, scalar2=None, op0=ALU.mult), waits=(t_consts,))
    t_carry0 = emit("V", lambda v: v.memset(carry[:], 0.0))
    t_half = emit("V", lambda v: v.memset(halfc[:], 0.5))
    t_prev0 = emit("V", lambda v: v.memset(A_(xn2T, 0, [[4104, 128], [513, CJ]]),
                                           0.0))

    stage_toks = [None, None]
    trbank_free = {0: None, 1: None}
    t_ffnw = None

    def emit_ffn_weights():
        nonlocal t_ffnw
        casts = []
        for i, w_src in enumerate((fk_d, fr_d, fv_d)):
            dma("G", wscr[i].ap(), w_src.ap(), sem_stg[0])
            casts.append(dtok(sem_stg[0]))
        for i, w_dst in enumerate((wkT, wrT, wvT)):
            for cjc in range(CJ):
                emit("X", lambda x, i=i, cjc=cjc, w_dst=w_dst:
                     x.dma_start(out=w_dst[:, cjc * 1024:(cjc + 1) * 1024],
                                 in_=A_(wscr[i], cjc * 128,
                                        [[DM, DM], [1, 128]]),
                                 transpose=True),
                     waits=(casts[i],), dma_sem=sem_stg[1])
                key = sem_stg[1].name if hasattr(sem_stg[1], "name") else id(sem_stg[1])
                dma_count[key] = dma_count.get(key, 0) + 1
        t_ffnw = dtok(sem_stg[1])

    bank_free = {0: None, 1: None, 2: None, 3: None}
    ST = {
        "scan_prev": None, "prevcol": t_prev0, "xn2T_last": t_prev0,
        "z_done": [None] * NTC, "z_tiles": [None] * NTC,
        "store_all": [None] * (NTC + 1),
        "xT_free": [t_consts, t_consts],
        "vv_last": [None] * NTC, "kk_last": [None] * NTC,
        "smm": [None] * NTC, "rmm": [None] * NTC,
        "t_y0": [None] * NTC, "t_q": [None] * NTC, "t_p": [None] * NTC,
        "t_u": [None] * NTC, "t_evv": [None] * NTC,
        "t_xr": [None] * NTC, "t_xk": [None] * NTC, "t_diff": [None] * NTC,
        "t_xkl": [None] * NTC, "t_xrl": [None] * NTC, "t_d_ev": None,
        "t_x2l": [None] * NTC, "tka_last": [None] * NTC,
        "t_f": [None] * NTC,
        "r_done": [None] * NTC, "xbeta": [None] * NTC,
        "t_x2": [None] * NTC, "t_c2": [None] * NTC,
    }

    def stats_base(tc):
        par = (tc % 2) * 76
        return par + 0, par + 24, par + 48, par + 72

    def xtile(tc, ts):
        return xsb[:, (tc % 2) * 4096 + ts * 1024:
                   (tc % 2) * 4096 + (ts + 1) * 1024]

    def colmath_pair(base, ts0, SCR, eps, wa, wb, split_sum=False):
        def c(k, base=base, ts0=ts0):
            return stats[:, base + k + ts0:base + k + ts0 + 2]
        scr = stats[:, SCR + ts0:SCR + ts0 + 2]
        if split_sum:
            t0 = emit("V", lambda v, c=c: v.tensor_add(
                c(0), c(0), c(4)), waits=tuple(wa))
            wa = (t0,)
        t_a = emit("V", lambda v, c=c, scr=scr: v.tensor_mul(
            scr, c(0), c(0)), waits=tuple(wa))
        t3a = emit("V", lambda v, c=c: v.tensor_add(
            c(8), c(8), c(12)), waits=tuple(wb))
        t_b = emit("V", lambda v, c=c, eps=eps: v.tensor_scalar(
            out=c(8), in0=c(8), scalar1=INV_D, scalar2=eps,
            op0=ALU.mult, op1=ALU.add), waits=(t3a,))
        t_c = emit("V", lambda v, c=c, scr=scr: v.scalar_tensor_tensor(
            out=c(8), in0=scr, scalar=-INV_D * INV_D, in1=c(8),
            op0=ALU.mult, op1=ALU.add), waits=(t_a, t_b))
        t_d = emit("V", lambda v, c=c, scr=scr: v.reciprocal(scr, c(8)),
                   waits=(t_c,))
        t_e = emit("S", lambda s, c=c, scr=scr: s.activation(
            c(16), scr, AF.Sqrt), waits=(t_d,))
        t_f = emit("V", lambda v, c=c: v.scalar_tensor_tensor(
            out=c(20), in0=c(0), scalar=-INV_D, in1=c(16),
            op0=ALU.mult, op1=ALU.mult), waits=(t_e,))
        return t_f

    def ln_stats(tc, base, SCR, eps, src_tile, extra_red=(), extra_sq=()):
        t_cm = []
        for half in range(2):
            tred, tsq = [], []
            for ts in (2 * half, 2 * half + 1):
                tred.append(emit("V", lambda v, a=src_tile(ts),
                                 c=stats[:, base + ts:base + ts + 1]:
                                 v.reduce_sum(c, a, axis=AX.X),
                                 waits=tuple(extra_red)))
                for hh in range(2):
                    tsq.append(emit("S", lambda s, a=src_tile(ts), hh=hh,
                                    c=stats[:, base + 8 + 4 * hh + ts:
                                            base + 9 + 4 * hh + ts]:
                                    s.activation(
                                        sq[:, 0:512],
                                        bass.AP(a.tensor, a.offset + hh * 512,
                                                [[a.ap[0][0], 128], [1, 512]]),
                                        AF.Square, accum_out=c),
                                    waits=tuple(extra_sq)))
            t_cm.append(colmath_pair(base, 2 * half, SCR, eps,
                                     (tred[-1],), (tsq[-1],)))
        return t_cm

    def phase_front_a(tc):
        slot = tc % 2
        ST1, STB, STC, SCR = stats_base(tc)
        if tc >= 2:
            t_x[tc] = load_x(tc, waitl=ST["z_tiles"][tc - 2])

        t_ln1 = ln_stats(tc, ST1, SCR, EPS_PRE, lambda ts: xtile(tc, ts),
                         extra_red=(t_x[tc],), extra_sq=(t_x[tc],))
        t_c = []
        for ts in range(NTS):
            wl = [t_ln1[ts // 2]] if ts % 2 == 0 else []
            if ts == 0 and tc > 0:
                wl.append(ST["t_f"][tc - 1])
            t_c.append(emit("S", lambda s, a=xtile(tc, ts), ts=ts,
                            bi=stats[:, ST1 + 20 + ts:ST1 + 21 + ts],
                            sc=stats[:, ST1 + 16 + ts:ST1 + 17 + ts]:
                            s.activation(xnb[:, ts * 1024:(ts + 1) * 1024], a,
                                         AF.Identity, bias=bi, scale=sc),
                            waits=tuple(wl)))
        t_xbeta = emit("G", lambda g, xoff=slot * 4096: g.tensor_add(
            xsb[:, xoff:xoff + 4096], xsb[:, xoff:xoff + 4096],
            A_(bb, 0, [[DM, 128], [0, NTS], [1, DM]])),
            waits=(t_c[-1], t_ln1[-1], t_gb, t_x[tc]))
        ST["xbeta"][tc] = t_xbeta
        ST["t_c1"] = t_c

    def phase_front_b(tc):
        slot = tc % 2
        xTb = slot * 4096
        t_c = ST["t_c1"]
        t_d_ev = []
        for cj in range(CJ):
            rot = cj % 2
            ttr = None
            for ts in range(NTS):
                wl = [t_c[ts]]
                if ts == 0 and trbank_free[rot] is not None:
                    wl.append(trbank_free[rot])
                ttr = emit("T", lambda t, cj=cj, ts=ts, rot=rot: t.transpose(
                    p_tr[rot][:, ts * 128:(ts + 1) * 128],
                    xnb[:, ts * 1024 + cj * 128: ts * 1024 + (cj + 1) * 128],
                    eyeb[:]), waits=tuple(wl))
            tev = emit("V", lambda v, cj=cj, rot=rot, xTb=xTb: v.tensor_copy(
                xT[:, xTb + cj * 512: xTb + (cj + 1) * 512],
                p_tr[rot][:, 0:512]),
                waits=(ttr, ST["r_done"][tc - 2] if tc >= 2 else None))
            trbank_free[rot] = tev
            t_d_ev.append(tev)
        ST["t_d"] = t_d_ev[-1]
        ST["t_d_ev"] = t_d_ev

        qk_prev_reader = ST["t_y0"][tc - 1] if tc > 0 else None
        aug_prev = ST["t_q"][tc - 1] if tc > 0 else None
        for pi, (wsb, bcol, dstt, doff, twt) in enumerate(
                ((wq, bqc, qkbuf, 0, t_wq), (wk, bkc, xnb, 0, t_wkv),
                 (wv, bvc, vsb, 0, t_wkv))):
            for jc in range(CI):
                bk = (pi * CI + jc) % 4
                tmm = None
                for cj in range(CJ):
                    wl = [ST["t_d_ev"][cj]]
                    if cj == 0:
                        wl.append(twt)
                        if bank_free[bk] is not None:
                            wl.append(bank_free[bk])
                    tmm = emit("T", lambda t, wsb=wsb, jc=jc, cj=cj, bk=bk,
                               xTb=xTb: t.matmul(
                                   p_mm[bk][:, 0:512],
                                   wsb[:, cj * 512 + jc * 128:
                                       cj * 512 + (jc + 1) * 128],
                                   xT[:, xTb + cj * 512: xTb + (cj + 1) * 512],
                                   start=(cj == 0), stop=(cj == CJ - 1)),
                               waits=tuple(wl))
                wl = [tmm]
                if pi == 0 and jc == 0 and qk_prev_reader is not None:
                    wl.append(qk_prev_reader)
                if pi == 2 and jc == 0 and aug_prev is not None:
                    wl.append(aug_prev)
                if pi == 1 and jc == 0 and tc > 0:
                    wl.append(ST["tka_last"][tc - 1])
                tev = emit("V", lambda v, dstt=dstt, doff=doff, jc=jc,
                           bk=bk, bcol=bcol: v.tensor_scalar(
                               out=dstt[:, doff + jc * 512:
                                        doff + (jc + 1) * 512],
                               in0=p_mm[bk][:, 0:512],
                               scalar1=bcol[:, jc:jc + 1], scalar2=None,
                               op0=ALU.add), waits=tuple(wl))
                bank_free[bk] = tev
                if pi == 0:
                    ST["t_evq"] = tev
                elif pi == 1:
                    ST["t_evk"] = tev
                else:
                    ST["t_evv"][tc] = tev

    def phase_tail2(tc):
        nonlocal p_sm_free, p_st_free
        xTb = (tc % 2) * 4096
        if tc > 0:
            ST["prevcol"] = emit("V", lambda v: v.tensor_copy(
                A_(xn2T, 0, [[4104, 128], [513, CJ]]),
                A_(xn2T, 512, [[4104, 128], [513, CJ]])),
                waits=(ST["xn2T_last"],))
        wbank = [p_sm, p_st]
        wfree = [p_sm_free, p_st_free]
        t_w_ev = []
        for cj in range(CJ):
            rot = cj % 2
            ttr = None
            for ts in range(NTS):
                wl = [ST["t_c2"][tc][ts]]
                if ts == 0 and wfree[rot] is not None:
                    wl.append(wfree[rot])
                ttr = emit("T", lambda t, cj=cj, ts=ts, rot=rot: t.transpose(
                    wbank[rot][:, ts * 128:(ts + 1) * 128],
                    aA[:, ts * 1024 + cj * 128: ts * 1024 + (cj + 1) * 128],
                    eyef[:]), waits=tuple(wl))
            tev = emit("S", lambda s, cj=cj, rot=rot: s.activation(
                xn2T[:, cj * 513 + 1: cj * 513 + 513], wbank[rot][:, 0:512],
                AF.Identity), waits=(ttr, ST["prevcol"]))
            wfree[rot] = tev
            t_w_ev.append(tev)
        p_sm_free, p_st_free = wfree
        ST["xn2T_last"] = t_w_ev[-1]

        t_diff = emit("V", lambda v: v.tensor_sub(
            aA[:, 0:4096], A_(xn2T, 1, [[4104, 128], [513, CJ], [1, 512]]),
            A_(xn2T, 0, [[4104, 128], [513, CJ], [1, 512]])),
            waits=(t_w_ev[-1],))
        ST["t_diff"][tc] = t_diff
        t_xk = []
        for cj in range(CJ):
            wl = [t_diff, t_consts]
            if cj == 0:
                wl.append(ST["kk_last"][tc - 1] if tc > 0 else t_ffnw)
            t_xk.append(emit("V", lambda v, cj=cj: v.scalar_tensor_tensor(
                out=bigb[:, cj * 512:(cj + 1) * 512],
                in0=aA[:, cj * 512:(cj + 1) * 512],
                scalar=mukc[:, cj:cj + 1],
                in1=A_(xn2T, cj * 513, [[4104, 128], [1, 512]]),
                op0=ALU.mult, op1=ALU.add), waits=tuple(wl)))
        ST["t_xk"][tc] = t_xk[-1]
        ST["t_xkl"][tc] = t_xk
        t_xr = []
        for cj in range(CJ):
            wl = [t_diff, t_consts]
            if cj == 0:
                wl.append(ST["t_evv"][tc])
            t_xr.append(emit("V", lambda v, cj=cj, xTb=xTb:
                        v.scalar_tensor_tensor(
                            out=xT[:, xTb + cj * 512: xTb + (cj + 1) * 512],
                            in0=aA[:, cj * 512:(cj + 1) * 512],
                            scalar=murc[:, cj:cj + 1],
                            in1=A_(xn2T, cj * 513, [[4104, 128], [1, 512]]),
                            op0=ALU.mult, op1=ALU.add), waits=tuple(wl)))
        ST["t_xr"][tc] = t_xr[-1]
        ST["t_xrl"][tc] = t_xr

    def phase_mid_a1(tc):
        nonlocal p_sm_free, p_st_free
        vvp = ST["vv_last"][tc - 2] if tc >= 2 else None

        t_f = emit("V", lambda v: v.tensor_mul(
            bigb[:, 4096:6144], xnb[:, 0:2048], qkbuf[:, 0:2048]),
            waits=(ST["t_evq"], ST["t_evk"], vvp))
        ST["t_f"][tc] = t_f

        tmm = None
        for ts in range(NTS):
            for ci in range(CI):
                wl = []
                if ts == 0 and ci == 0:
                    wl = [t_f, t_consts]
                    if p_st_free is not None:
                        wl.append(p_st_free)
                tmm = emit("T", lambda t, ts=ts, ci=ci: t.matmul(
                    p_st[:, ts * 8:(ts + 1) * 8],
                    bigb[:, 4096 + ci * 512 + ts * 128:
                         4096 + ci * 512 + (ts + 1) * 128],
                    sels[:, ci * 8:(ci + 1) * 8],
                    start=(ci == 0), stop=(ci == CI - 1)), waits=tuple(wl))
        ST["smm"][tc] = tmm
        t_j = emit("V", lambda v: v.reduce_max(
            mx4[:], A_(p_st, 0, [[512, 128], [8, 4], [1, 8]]), axis=AX.X,
            negate=True), waits=(tmm,))
        texp = None
        for g in range(4):
            texp = emit("S", lambda s, g=g: s.activation(
                stm[:, g * 8:(g + 1) * 8], p_st[:, g * 8:(g + 1) * 8],
                AF.Exp, bias=mx4[:, g:g + 1]),
                waits=(t_j,) if g == 0 else ())
        p_st_free = texp
        t_j = emit("V", lambda v: v.reduce_sum(
            rc4[:], A_(stm, 0, [[32, 128], [8, 4], [1, 8]]), axis=AX.X),
            waits=(texp,))
        t_j = emit("V", lambda v: v.reciprocal(rc4[:], rc4[:]), waits=(t_j,))
        t_j = emit("V", lambda v: v.tensor_mul(
            A_(stm, 0, [[32, 128], [8, 4], [1, 8]]),
            A_(stm, 0, [[32, 128], [8, 4], [1, 8]]),
            A_(rc4, 0, [[4, 128], [1, 4], [0, 8]])), waits=(t_j,))
        ST["t_jn"] = t_j
        ST["texp"] = texp
        p_st_free = texp

    def phase_mid_a2(tc):
        nonlocal p_sm_free, p_st_free
        t_j = ST["t_jn"]
        ttr = None
        for ts in range(NTS):
            wl = []
            if ts == 0:
                wl = [t_j]
                if p_sm_free is not None:
                    wl.append(p_sm_free)
            ttr = emit("T", lambda t, ts=ts: t.transpose(
                p_sm[0:8, ts * 128:(ts + 1) * 128], stm[:, ts * 8:(ts + 1) * 8],
                eyef[:]), waits=tuple(wl))
        p_sm_free = emit("V", lambda v: v.tensor_copy(arow[:],
                                                      p_sm[0:8, 0:512]),
                         waits=(ttr,))
        t_l = p_sm_free

        t_u = []
        for ci in range(CI):
            bk = 2 + ci % 2
            wl = [t_l, t_consts]
            if bank_free[bk] is not None:
                wl.append(bank_free[bk])
            tmm = emit("T", lambda t, ci=ci, bk=bk: t.matmul(
                p_mm[bk][:, 0:512], selb[0:8, ci * 128:(ci + 1) * 128],
                arow[0:8, 0:512], start=True, stop=True), waits=tuple(wl))
            wl = [tmm, ST["t_evv"][tc], t_decay]
            if ci == 0 and tc > 0:
                wl.append(ST["t_q"][tc - 1])
            tn = emit("V", lambda v, ci=ci, bk=bk: v.scalar_tensor_tensor(
                out=aB[:, ci * 512:(ci + 1) * 512], in0=p_mm[bk][:, 0:512],
                scalar=omd[:, ci:ci + 1], in1=vsb[:, ci * 512:(ci + 1) * 512],
                op0=ALU.mult, op1=ALU.mult), waits=tuple(wl))
            bank_free[bk] = tn
            t_u.append(tn)
        ST["t_u"][tc] = t_u[-1]

        t_scan = []
        for ci in range(CI):
            wl = [t_u[ci]]
            if tc == 0 and ci == 0:
                wl.append(t_carry0)
            t_scan.append(emit("V", lambda v, ci=ci: v.tensor_tensor_scan(
                out=aB[:, ci * 512:(ci + 1) * 512],
                data0=A_(decay, ci, [[4, 128], [0, 512]]),
                data1=aB[:, ci * 512:(ci + 1) * 512],
                initial=carry[:, ci:ci + 1], op0=ALU.mult, op1=ALU.add),
                waits=tuple(wl)))
        ST["scan_prev"] = emit("V", lambda v: v.tensor_copy(
            carry[:, 0:4], A_(aB, 511, [[2048, 128], [512, 4]])),
            waits=(t_scan[-1],))

        t_p = []
        for ci in range(CI):
            t_p.append(emit("S", lambda s, ci=ci: s.activation(
                vsb[:, ci * 512:(ci + 1) * 512],
                qkbuf[:, ci * 512:(ci + 1) * 512],
                AF.Tanh, scale=augph[:, ci:ci + 1]),
                waits=(t_augph, t_u[ci])))
        ST["t_p"][tc] = t_p[-1]
        t_y0 = emit("V", lambda v: v.tensor_mul(
            aB[:, 0:2048], aB[:, 0:2048], qkbuf[:, 0:2048]),
            waits=(ST["scan_prev"],))
        ST["t_y0"][tc] = t_y0
    def phase_mid_b(tc):
        t_q = emit("V", lambda v: v.scalar_tensor_tensor(
            out=bigb[:, 6144:8192], in0=vsb[:, 0:2048], scalar=0.5,
            in1=aB[:, 0:2048], op0=ALU.mult, op1=ALU.add),
            waits=(ST["t_p"][tc], ST["t_y0"][tc],
                   ST["vv_last"][tc - 1] if tc > 0 else None))
        ST["t_q"][tc] = t_q

    def phase_mid_r(tc):
        ST1, STB, STC, SCR = stats_base(tc)
        t_q = ST["t_q"][tc]
        t_oev = []
        for ts in range(NTS):
            for jh in range(2):
                bk = (ts * 2 + jh) % 4
                tmm = None
                for ic in range(CI):
                    wl = []
                    if ic == 0:
                        wl = [t_q, t_bff]
                        if bank_free[bk] is not None:
                            wl.append(bank_free[bk])
                    tmm = emit("T", lambda t, ts=ts, jh=jh, ic=ic, bk=bk:
                               t.matmul(p_mm[bk][:, 0:512],
                                        bigb[:, 6144 + ic * 512 + ts * 128:
                                             6144 + ic * 512 + (ts + 1) * 128],
                                        wo[:, ic * 1024 + jh * 512:
                                           ic * 1024 + (jh + 1) * 512],
                                        start=(ic == 0), stop=False),
                               waits=tuple(wl))
                tmm = emit("T", lambda t, jh=jh, bk=bk: t.matmul(
                    p_mm[bk][:, 0:512], ones1b[0:1, 0:128],
                    bffr[0:1, jh * 512:(jh + 1) * 512], start=False, stop=True))
                wl = [tmm]
                if ts == 0 and jh == 0 and ST["store_all"][tc - 1] is not None:
                    wl.append(ST["store_all"][tc - 1])
                tev = emit("S", lambda s, ts=ts, jh=jh, bk=bk,
                           c=stats[:, STB + 4 * jh + ts:STB + 1 + 4 * jh + ts]:
                           s.activation(
                               aA[:, ts * 1024 + jh * 512:
                                  ts * 1024 + (jh + 1) * 512],
                               p_mm[bk][:, 0:512], AF.Identity, accum_out=c),
                           waits=tuple(wl))
                bank_free[bk] = tev
                t_oev.append(tev)
        ST["rmm"][tc] = tmm
        ST["t_oev"] = t_oev

    def phase_tail1_s1(tc, half):
        ST1, STB, STC, SCR = stats_base(tc)
        t_oev = ST["t_oev"]
        if half == 0:
            ST["t_c2"][tc] = [None] * NTS
            ST["t_x2l"][tc] = [None] * NTS
            ST["t_t1"] = [None] * NTS
        p0 = 2 * half
        tsq = []
        for ts in (p0, p0 + 1):
            for hh in range(2):
                tsq.append(emit("S", lambda s, ts=ts, hh=hh,
                                c=stats[:, STB + 8 + 4 * hh + ts:
                                        STB + 9 + 4 * hh + ts]:
                                s.activation(
                                    sq[:, 0:512],
                                    aA[:, ts * 1024 + hh * 512:
                                       ts * 1024 + (hh + 1) * 512],
                                    AF.Square, accum_out=c),
                                waits=(t_oev[2 * ts + hh],)))
        t_lnb = colmath_pair(STB, p0, SCR, EPS_ATTN,
                             (t_oev[2 * p0 + 3],), (tsq[-1],),
                             split_sum=True)
        for ts in (p0, p0 + 1):
            ST["t_t1"][ts] = emit("S", lambda s, ts=ts,
                                  bi=stats[:, STB + 20 + ts:STB + 21 + ts],
                                  sc=stats[:, STB + 16 + ts:STB + 17 + ts]:
                                  s.activation(
                                      aA[:, ts * 1024:(ts + 1) * 1024],
                                      aA[:, ts * 1024:(ts + 1) * 1024],
                                      AF.Identity, bias=bi, scale=sc),
                                  waits=(t_lnb,) if ts == p0 else ())

    def phase_tail1_s2(tc, half):
        ST1, STB, STC, SCR = stats_base(tc)
        t_c2 = ST["t_c2"][tc]
        x2_all = ST["t_x2l"][tc]
        t_x2 = None
        p0 = 2 * half
        x2l = {}
        for ts in (p0, p0 + 1):
            tg = emit("V", lambda v, ts=ts: v.tensor_mul(
                aA[:, ts * 1024:(ts + 1) * 1024],
                aA[:, ts * 1024:(ts + 1) * 1024], gb[:, 0:DM]),
                waits=(ST["t_t1"][ts], t_gb))
            t_x2 = emit("V", lambda v, a=xtile(tc, ts), ts=ts,
                        c=stats[:, STC + ts:STC + ts + 1]:
                        v.scalar_tensor_tensor(
                            out=a, in0=aA[:, ts * 1024:(ts + 1) * 1024],
                            scalar=1.0, in1=a, op0=ALU.mult, op1=ALU.add,
                            accum_out=c),
                        waits=(tg, ST["xbeta"][tc]))
            x2l[ts] = t_x2
            x2_all[ts] = t_x2
        tsq = []
        for ts in (p0, p0 + 1):
            for hh in range(2):
                tsq.append(emit("S", lambda s, ts=ts, hh=hh,
                                a=xtile(tc, ts),
                                c=stats[:, STC + 8 + 4 * hh + ts:
                                        STC + 9 + 4 * hh + ts]:
                                s.activation(
                                    sq[:, 0:512],
                                    bass.AP(a.tensor, a.offset + hh * 512,
                                            [[a.ap[0][0], 128], [1, 512]]),
                                    AF.Square, accum_out=c),
                                waits=(x2l[ts],)))
        t_lnc = colmath_pair(STC, p0, SCR, EPS_PRE,
                             (t_x2,), (tsq[-1],))
        for ts in (p0, p0 + 1):
            t_c2[ts] = emit("S", lambda s, a=xtile(tc, ts), ts=ts,
                            bi=stats[:, STC + 20 + ts:STC + 21 + ts],
                            sc=stats[:, STC + 16 + ts:STC + 17 + ts]:
                            s.activation(
                                aA[:, ts * 1024:(ts + 1) * 1024], a,
                                AF.Identity, bias=bi, scale=sc),
                            waits=(t_lnc,) if ts == p0 else ())
        ST["t_x2"][tc] = t_x2

    def phase_ffn(tc):
        xTb = (tc % 2) * 4096
        t_ka = []
        trl_last = None
        for dj in range(CJ):
            bk = dj % 2
            tmm = None
            for cjc in range(CJ):
                wl = [ST["t_xkl"][tc][cjc]]
                if cjc == 0:
                    wl.append(t_ffnw)
                    if bank_free[bk] is not None:
                        wl.append(bank_free[bk])
                tmm = emit("T", lambda t, dj=dj, cjc=cjc, bk=bk: t.matmul(
                    p_mm[bk][:, 0:512],
                    wkT[:, cjc * 1024 + dj * 128: cjc * 1024 + (dj + 1) * 128],
                    bigb[:, cjc * 512:(cjc + 1) * 512],
                    start=(cjc == 0), stop=(cjc == CJ - 1)), waits=tuple(wl))
            trl = emit("V", lambda v, bk=bk, dj=dj: v.tensor_scalar_max(
                xnb[:, (dj % 2) * 512:(dj % 2 + 1) * 512], p_mm[bk][:, 0:512],
                0.0), waits=(tmm, ST["t_f"][min(tc + 1, NTC - 1)]))
            trl_last = trl
            wl = [trl]
            if dj < 4:
                wl.append(ST["smm"][min(tc + 1, NTC - 1)])
            else:
                wl.append(ST["rmm"][tc])
            tka = emit("S", lambda s, dj=dj: s.activation(
                bigb[:, 4096 + dj * 512: 4096 + (dj + 1) * 512],
                xnb[:, (dj % 2) * 512:(dj % 2 + 1) * 512], AF.Square),
                waits=tuple(wl))
            bank_free[bk] = tka
            t_ka.append(tka)
        kk_mm_last = tmm
        ST["tka_last"][tc] = t_ka[-1]

        ST["kk_last"][tc] = kk_mm_last

        last_tz = None
        tz3 = None
        z_tiles = []
        t_sg = []
        for ts in range(NTS):
            t_tr = []
            for dh in range(2):
                bk = 2 + (ts * 2 + dh) % 2
                tmm = None
                for cjc in range(CJ):
                    wl = [ST["t_xrl"][tc][cjc]]
                    if cjc == 0:
                        wl.append(t_ffnw)
                        if bank_free[bk] is not None:
                            wl.append(bank_free[bk])
                    tmm = emit("T", lambda t, ts=ts, dh=dh, cjc=cjc, bk=bk,
                               xTb=xTb: t.matmul(
                                   p_mm[bk][:, 0:512],
                                   xT[:, xTb + cjc * 512 + ts * 128:
                                      xTb + cjc * 512 + (ts + 1) * 128],
                                   wrT[:, cjc * 1024 + dh * 512:
                                       cjc * 1024 + (dh + 1) * 512],
                                   start=(cjc == 0), stop=(cjc == CJ - 1)),
                               waits=tuple(wl))
                wl = [tmm, ST["t_xk"][tc], ST["t_diff"][tc],
                      ST["t_c2"][tc][-1]]
                ttr2 = emit("S", lambda s, ts=ts, dh=dh, bk=bk: s.activation(
                    aA[:, ts * 1024 + dh * 512: ts * 1024 + (dh + 1) * 512],
                    p_mm[bk][:, 0:512], AF.Tanh, scale=0.5), waits=tuple(wl))
                bank_free[bk] = ttr2
                t_tr.append(ttr2)
            sg = emit("S", lambda s, ts=ts: s.activation(
                aA[:, ts * 1024:(ts + 1) * 1024],
                aA[:, ts * 1024:(ts + 1) * 1024],
                AF.Identity, bias=halfc[:, 0:1], scale=halfc[:, 0:1]),
                waits=(t_tr[1], t_half))
            t_sg.append(sg)
            for ih in range(2):
                bk = (ts * 2 + ih) % 2
                tmm = None
                for dj in range(CJ):
                    wl = []
                    if dj == 0:
                        wl = [t_ka[-1], t_ffnw]
                        if bank_free[bk] is not None:
                            wl.append(bank_free[bk])
                    tmm = emit("T", lambda t, ts=ts, ih=ih, dj=dj, bk=bk:
                               t.matmul(
                                   p_mm[bk][:, 0:512],
                                   bigb[:, 4096 + dj * 512 + ts * 128:
                                        4096 + dj * 512 + (ts + 1) * 128],
                                   wvT[:, dj * 1024 + ih * 512:
                                       dj * 1024 + (ih + 1) * 512],
                                   start=(dj == 0), stop=(dj == CJ - 1)),
                               waits=tuple(wl))
                ST["vv_last"][tc] = tmm
                tz = emit("V", lambda v, ts=ts, ih=ih, bk=bk: v.tensor_mul(
                    aA[:, ts * 1024 + ih * 512: ts * 1024 + (ih + 1) * 512],
                    p_mm[bk][:, 0:512],
                    aA[:, ts * 1024 + ih * 512: ts * 1024 + (ih + 1) * 512]),
                    waits=(tmm, sg))
                bank_free[bk] = tz
                last_tz = tz
            tz3 = emit("V", lambda e, ts=ts, a=xtile(tc, ts): e.tensor_add(
                aA[:, ts * 1024:(ts + 1) * 1024],
                aA[:, ts * 1024:(ts + 1) * 1024], a),
                waits=(last_tz, ST["t_x2l"][tc][ts]))
            z_tiles.append(tz3)
            out_ap = A_(y_d, (tc * TCW + ts * 128) * DM, [[DM, 128], [1, DM]])
            dma("X", out_ap, aA[:, ts * 1024:(ts + 1) * 1024], sem_do,
                waits=(tz3,))
        ST["r_done"][tc] = t_sg[-1]
        ST["z_done"][tc] = tz3
        ST["z_tiles"][tc] = z_tiles
        ST["store_all"][tc] = dtok(sem_do)

    phase_front_a(0)
    emit_ffn_weights()
    phase_front_b(0)
    phase_mid_a1(0)
    phase_mid_a2(0)
    for tc in range(NTC):
        if tc > 0:
            phase_tail2(tc - 1)
            phase_ffn(tc - 1)
        phase_mid_b(tc)
        phase_mid_r(tc)
        if tc + 1 < NTC:
            phase_front_a(tc + 1)
            phase_front_b(tc + 1)
        phase_tail1_s1(tc, 0)
        phase_tail1_s1(tc, 1)
        if tc + 1 < NTC:
            phase_mid_a1(tc + 1)
        phase_tail1_s2(tc, 0)
        phase_tail1_s2(tc, 1)
        if tc + 1 < NTC:
            phase_mid_a2(tc + 1)
    phase_tail2(NTC - 1)
    phase_ffn(NTC - 1)

    emit("X", None, waits=(dtok(sem_do),))

    marks = {e: set() for e in ENGS}
    for e in ENGS:
        for ent in sched[e]:
            for w in ent["waits"]:
                if w[0] == "E":
                    marks[w[1]].add(w[2])
    mile = {}
    for e in ENGS:
        mile[e] = {idx: n + 1 for n, idx in enumerate(sorted(marks[e]))}

    def runner(e):
        def run(eng):
            waited = {}
            for idx, ent in enumerate(sched[e]):
                for w in ent["waits"]:
                    if w[0] == "E":
                        _, pe, pidx = w
                        val = mile[pe][pidx]
                        key = "E" + pe
                        if waited.get(key, 0) < val:
                            eng.wait_ge(esem[pe], val)
                            waited[key] = val
                    else:
                        _, dsem, val = w
                        key = dsem.name if hasattr(dsem, "name") else id(dsem)
                        if waited.get(("D", key), 0) < val:
                            eng.wait_ge(dsem, val)
                            waited[("D", key)] = val
                if ent["fn"] is None:
                    continue
                inst = ent["fn"](eng)
                if ent["dma_sem"] is not None:
                    inst = inst.then_inc(ent["dma_sem"], 16)
                if idx in mile[e]:
                    inst.then_inc(esem[e], 1)
        return run

    with es:
        with nc.Block() as blk:
            blk.tensor(runner("T"))
            blk.vector(runner("V"))
            blk.scalar(runner("S"))
            blk.gpsimd(runner("G"))
            blk.sync(runner("X"))
    return nc


_NC_CACHE = None


def kernel(L0_attn_x, L0_attn_Wq, L0_attn_bq, L0_attn_Wk, L0_attn_bk,
           L0_attn_Wv, L0_attn_bv, L0_attn_Wo, L0_attn_bo, L0_attn_alpha,
           L0_attn_aug_param, L0_attn_ln_scale, L0_attn_ln_bias,
           L0_ffn_W_r, L0_ffn_W_k, L0_ffn_W_v, L0_ffn_mu_r, L0_ffn_mu_k):
    global _NC_CACHE
    if _NC_CACHE is None:
        _NC_CACHE = build_nc()
    nc = _NC_CACHE

    c = np.ascontiguousarray
    f32 = np.float32

    def cols(vec, k):
        return c(np.asarray(vec, f32).reshape(k, 128).T)

    base = {
        "wq": c(np.asarray(L0_attn_Wq, f32)),
        "wk": c(np.asarray(L0_attn_Wk, f32)),
        "wv": c(np.asarray(L0_attn_Wv, f32)),
        "wo": c(np.asarray(L0_attn_Wo, f32)),
        "fr": c(np.asarray(L0_ffn_W_r, f32)),
        "fk": c(np.asarray(L0_ffn_W_k, f32)),
        "fv": c(np.asarray(L0_ffn_W_v, f32)),
        "bqc": cols(L0_attn_bq, 4),
        "bkc": cols(L0_attn_bk, 4),
        "bvc": cols(L0_attn_bv, 4),
        "alc": cols(np.asarray(L0_attn_alpha, f32).reshape(512), 4),
        "apc": cols(np.asarray(L0_attn_aug_param, f32).reshape(512), 4),
        "murc": cols(L0_ffn_mu_r, 8),
        "mukc": cols(L0_ffn_mu_k, 8),
        "grow": c(np.asarray(L0_attn_ln_scale, f32).reshape(1, DM)),
        "brow": c(np.asarray(L0_attn_ln_bias, f32).reshape(1, DM)),
        "borow": c(np.asarray(L0_attn_bo, f32).reshape(1, DM)),
        "eyeb": np.eye(128, dtype=BFnp),
        "eyef": np.eye(128, dtype=f32),
        "ones1b": np.ones((1, 128), BFnp),
        "ones1f": np.ones((1, 128), f32),
        "onescb": np.ones((128, 1), BFnp),
    }
    sels = np.zeros((128, 32), f32)
    selb = np.zeros((8, 512), f32)
    for ci in range(4):
        for p in range(128):
            h = 2 * ci + (1 if p >= 64 else 0)
            sels[p, ci * 8 + h] = 0.125
            selb[h, ci * 128 + p] = 1.0
    base["sels"] = sels.astype(BFnp)
    base["selb"] = selb

    xs = np.asarray(L0_attn_x, f32)
    in_maps = []
    for b in range(8):
        m = dict(base)
        m["x"] = c(xs[b])
        in_maps.append(m)

    res = run_bass_kernel_spmd(nc, in_maps, core_ids=list(range(8)))
    out = np.stack([np.asarray(r["y"]) for r in res.results], axis=0)
    return out.astype(np.float32)


# version 19
# speedup vs baseline: 1.0944x; 1.0285x over previous
import contextlib

import numpy as np
import ml_dtypes

import concourse.bass as bass
import concourse.mybir as mybir
from concourse.bass_utils import run_bass_kernel_spmd

F32 = mybir.dt.float32
BF16 = mybir.dt.bfloat16
AF = mybir.ActivationFunctionType
ALU = mybir.AluOpType
AX = mybir.AxisListType
BFnp = ml_dtypes.bfloat16

SEQ, DM, INNER = 2048, 1024, 512
NTC, TCW = 4, 512
NTS = 4
CJ = 8
CI = 4
EPS_PRE = 1e-5
EPS_ATTN = 1e-6
INV_D = 1.0 / DM


def build_nc():
    nc = bass.Bass("TRN2", target_bir_lowering=False,
                   dynamic_dma_scratch_size=8192,
                   detect_race_conditions=False)

    x_d = nc.dram_tensor("x", [SEQ, DM], F32, kind="ExternalInput")
    wq_d = nc.dram_tensor("wq", [DM, INNER], F32, kind="ExternalInput")
    wk_d = nc.dram_tensor("wk", [DM, INNER], F32, kind="ExternalInput")
    wv_d = nc.dram_tensor("wv", [DM, INNER], F32, kind="ExternalInput")
    wo_d = nc.dram_tensor("wo", [INNER, DM], F32, kind="ExternalInput")
    fr_d = nc.dram_tensor("fr", [DM, DM], F32, kind="ExternalInput")
    fk_d = nc.dram_tensor("fk", [DM, DM], F32, kind="ExternalInput")
    fv_d = nc.dram_tensor("fv", [DM, DM], F32, kind="ExternalInput")
    bqc_d = nc.dram_tensor("bqc", [128, 4], F32, kind="ExternalInput")
    bkc_d = nc.dram_tensor("bkc", [128, 4], F32, kind="ExternalInput")
    bvc_d = nc.dram_tensor("bvc", [128, 4], F32, kind="ExternalInput")
    alc_d = nc.dram_tensor("alc", [128, 4], F32, kind="ExternalInput")
    apc_d = nc.dram_tensor("apc", [128, 4], F32, kind="ExternalInput")
    murc_d = nc.dram_tensor("murc", [128, 8], F32, kind="ExternalInput")
    mukc_d = nc.dram_tensor("mukc", [128, 8], F32, kind="ExternalInput")
    grow_d = nc.dram_tensor("grow", [1, DM], F32, kind="ExternalInput")
    brow_d = nc.dram_tensor("brow", [1, DM], F32, kind="ExternalInput")
    borow_d = nc.dram_tensor("borow", [1, DM], F32, kind="ExternalInput")
    eyeb_d = nc.dram_tensor("eyeb", [128, 128], BF16, kind="ExternalInput")
    eyef_d = nc.dram_tensor("eyef", [128, 128], F32, kind="ExternalInput")
    sels_d = nc.dram_tensor("sels", [128, 32], BF16, kind="ExternalInput")
    selb_d = nc.dram_tensor("selb", [8, 512], F32, kind="ExternalInput")
    ones1b_d = nc.dram_tensor("ones1b", [1, 128], BF16, kind="ExternalInput")
    ones1f_d = nc.dram_tensor("ones1f", [1, 128], F32, kind="ExternalInput")
    onescb_d = nc.dram_tensor("onescb", [128, 1], BF16, kind="ExternalInput")
    y_d = nc.dram_tensor("y", [SEQ, DM], F32, kind="ExternalOutput")
    wscr = [nc.dram_tensor(f"wscr{i}", [DM, DM], BF16, kind="Internal")
            for i in range(3)]

    es = contextlib.ExitStack()

    def sb(name, shape, dtype):
        return es.enter_context(nc.sbuf_tensor(name, shape, dtype))

    def ps(name, shape, dtype):
        return es.enter_context(nc.psum_tensor(name, shape, dtype))

    def sem(name):
        return es.enter_context(nc.semaphore(name))

    wq = sb("wq_sb", [128, 4096], BF16)
    wk = sb("wk_sb", [128, 4096], BF16)
    wv = sb("wv_sb", [128, 4096], BF16)
    wo = sb("wo_sb", [128, 4096], BF16)
    wrT = sb("wrT", [128, 8192], BF16)
    wkT = sb("wkT", [128, 8192], BF16)
    wvT = sb("wvT", [128, 8192], BF16)
    xsb = sb("xsb", [128, 8192], F32)
    xnb = sb("xnb", [128, 4096], BF16)
    xT = sb("xT", [128, 8192], BF16)
    bigb = sb("bigb", [128, 8192], BF16)
    qkbuf = sb("qkbuf", [128, 2048], BF16)
    aA = sb("aA", [128, 4096], F32)
    aB = sb("aB", [128, 2048], F32)
    vsb = sb("vsb", [128, 2048], F32)
    xn2T = sb("xn2T", [128, 4104], BF16)
    gb = sb("gb", [128, DM], F32)
    bb = sb("bb", [128, DM], F32)
    sq = sb("sq", [128, 512], F32)
    stats = sb("stats", [128, 152], F32)
    carry = sb("carry", [128, 4], F32)
    decay = sb("decay", [128, 4], F32)
    omd = sb("omd", [128, 4], F32)
    augph = sb("augph", [128, 4], F32)
    s8 = sb("s8", [8, 512], F32)
    arow = s8
    selb = sb("selb_sb", [8, 512], F32)
    stm = sb("stm", [128, 32], F32)
    atm = stm
    mx4 = sb("mx4", [128, 4], F32)
    rc4 = sb("rc4", [128, 4], F32)
    halfc = sb("halfc", [128, 1], F32)
    bffr = sb("bffr", [1, DM], BF16)
    eyeb = sb("eyeb_sb", [128, 128], BF16)
    eyef = sb("eyef_sb", [128, 128], F32)
    sels = sb("sels_sb", [128, 32], BF16)
    ones1b = sb("ones1b_sb", [1, 128], BF16)
    ones1f = sb("ones1f_sb", [1, 128], F32)
    onescb = sb("onescb_sb", [128, 1], BF16)
    bqc = sb("bqc_sb", [128, 4], F32)
    bkc = sb("bkc_sb", [128, 4], F32)
    bvc = sb("bvc_sb", [128, 4], F32)
    alc = sb("alc_sb", [128, 4], F32)
    apc = sb("apc_sb", [128, 4], F32)
    murc = sb("murc_sb", [128, 8], F32)
    mukc = sb("mukc_sb", [128, 8], F32)

    p_mm = [ps("p_mm0", [128, 512], F32), ps("p_mm1", [128, 512], F32),
            ps("p_mm2", [128, 512], F32), ps("p_mm3", [128, 512], F32)]
    p_tr = [ps("p_tr0", [128, 512], BF16), ps("p_tr1", [128, 512], BF16)]
    p_sm = ps("p_sm", [128, 512], F32)
    p_st = ps("p_st", [128, 512], F32)

    ENGS = ("T", "V", "S", "G", "X")
    esem = {e: sem("sem_" + e) for e in ENGS}
    sem_dc = sem("sem_dc")
    sem_dx = [sem("sem_dx0"), sem("sem_dx1")]
    sem_do = sem("sem_do")
    sem_wq = sem("sem_wq")
    sem_wkv = sem("sem_wkv")
    sem_wo = sem("sem_wo")
    sem_stg = [sem("sem_stg0"), sem("sem_stg1")]

    sched = {e: [] for e in ENGS}
    dma_count = {}

    def emit(e, fn, waits=(), dma_sem=None):
        idx = len(sched[e])
        wl = []
        for w in waits:
            if w is None:
                continue
            if w[0] == "D":
                wl.append(w)
            else:
                pe, pidx = w
                assert pidx < len(sched[pe]) or (pe == e and pidx < idx), \
                    "wait on future entry"
                if pe == e and idx - pidx >= 16:
                    continue
                wl.append(("E", pe, pidx))
        sched[e].append({"fn": fn, "waits": wl, "dma_sem": dma_sem})
        return (e, idx)

    def dma(e, out_ap, in_ap, dsem, waits=()):
        key = dsem.name if hasattr(dsem, "name") else id(dsem)
        dma_count[key] = dma_count.get(key, 0) + 1
        emit(e, lambda eng, o=out_ap, i=in_ap: eng.dma_start(out=o, in_=i),
             waits=waits, dma_sem=dsem)

    def dtok(dsem):
        key = dsem.name if hasattr(dsem, "name") else id(dsem)
        return ("D", dsem, 16 * dma_count.get(key, 0))

    A_ = bass.AP

    for dst, src in ((eyeb, eyeb_d), (eyef, eyef_d), (sels, sels_d),
                     (ones1b, ones1b_d), (ones1f, ones1f_d),
                     (onescb, onescb_d), (bqc, bqc_d), (bkc, bkc_d),
                     (bvc, bvc_d), (alc, alc_d), (apc, apc_d),
                     (murc, murc_d), (mukc, mukc_d)):
        dma("X", dst[:], src.ap(), sem_dc)
    dma("X", selb[:], selb_d.ap(), sem_dc)
    dma("X", vsb[0:1, 1024:2048], grow_d.ap(), sem_dc)
    dma("X", aB[0:1, 0:1024], brow_d.ap(), sem_dc)
    dma("X", aB[0:1, 1024:2048], borow_d.ap(), sem_dc)
    t_consts = dtok(sem_dc)

    def load_x(tc, waitl=None):
        slot = tc % 2
        for ts in range(NTS):
            in_ap = A_(x_d, (tc * TCW + ts * 128) * DM, [[DM, 128], [1, DM]])
            dma("X", xsb[:, slot * 4096 + ts * 1024:
                         slot * 4096 + (ts + 1) * 1024],
                in_ap, sem_dx[slot],
                waits=(waitl[ts],) if waitl is not None else ())
        return dtok(sem_dx[slot])

    t_x = [None] * NTC
    t_x[0] = load_x(0)
    t_x[1] = load_x(1)

    def cast_w_attn(dst, src, dsem):
        in_ap = A_(src, 0, [[INNER, 128], [128 * INNER, CJ], [1, INNER]])
        out_ap = A_(dst, 0, [[4096, 128], [512, CJ], [1, 512]])
        dma("G", out_ap, in_ap, dsem)

    cast_w_attn(wq, wq_d, sem_wq)
    t_wq = dtok(sem_wq)
    cast_w_attn(wk, wk_d, sem_wkv)
    cast_w_attn(wv, wv_d, sem_wkv)
    t_wkv = dtok(sem_wkv)
    dma("G", A_(wo, 0, [[4096, 128], [1024, CI], [1, DM]]),
        A_(wo_d, 0, [[DM, 128], [128 * DM, CI], [1, DM]]), sem_wo)
    t_wo = dtok(sem_wo)

    t_gb = None
    for half in range(2):
        for rowt, roff, dest in ((vsb, 1024, gb), (aB, 0, bb)):
            tmm = emit("T", lambda t, rowt=rowt, roff=roff, h=half: t.matmul(
                p_st[:, 0:512], ones1f[0:1, 0:128],
                rowt[0:1, roff + h * 512: roff + (h + 1) * 512],
                start=True, stop=True), waits=(t_consts, t_gb))
            t_gb = emit("V", lambda v, d=dest, h=half: v.tensor_copy(
                d[:, h * 512:(h + 1) * 512], p_st[:, 0:512]), waits=(tmm,))
    p_st_free = t_gb

    t_bff = None
    for jh in range(2):
        tmm = None
        for ic in range(CI):
            tmm = emit("T", lambda t, ic=ic, jh=jh: t.matmul(
                p_sm[0:1, 0:512], onescb[:, 0:1],
                wo[:, ic * 1024 + jh * 512: ic * 1024 + (jh + 1) * 512],
                start=(ic == 0), stop=(ic == CI - 1)),
                waits=(t_wo, t_consts, t_bff) if ic == 0 else ())
        t_bff = emit("V", lambda v, jh=jh: v.scalar_tensor_tensor(
            out=bffr[0:1, jh * 512:(jh + 1) * 512], in0=p_sm[0:1, 0:512],
            scalar=0.5, in1=aB[0:1, 1024 + jh * 512: 1024 + (jh + 1) * 512],
            op0=ALU.mult, op1=ALU.add), waits=(tmm, t_consts))
    p_sm_free = t_bff

    tdk = emit("S", lambda s: s.activation(stats[:, 72:76], alc[:], AF.Tanh,
                                           scale=0.5), waits=(t_consts,))
    tdk = emit("V", lambda v: v.tensor_scalar(out=decay[:], in0=stats[:, 72:76],
               scalar1=0.5, scalar2=0.5, op0=ALU.mult, op1=ALU.add),
               waits=(tdk,))
    t_decay = emit("V", lambda v: v.tensor_scalar(out=omd[:], in0=stats[:, 72:76],
                   scalar1=-0.5, scalar2=0.5, op0=ALU.mult, op1=ALU.add),
                   waits=(tdk,))
    t_augph = emit("V", lambda v: v.tensor_scalar(out=augph[:], in0=apc[:],
                   scalar1=0.5, scalar2=None, op0=ALU.mult), waits=(t_consts,))
    t_carry0 = emit("V", lambda v: v.memset(carry[:], 0.0))
    t_half = emit("V", lambda v: v.memset(halfc[:], 0.5))
    t_prev0 = emit("V", lambda v: v.memset(A_(xn2T, 0, [[4104, 128], [513, CJ]]),
                                           0.0))

    stage_toks = [None, None]
    trbank_free = {0: None, 1: None}
    t_ffnw = None

    def emit_ffn_weights():
        nonlocal t_ffnw
        casts = []
        for i, w_src in enumerate((fk_d, fr_d, fv_d)):
            dma("G", wscr[i].ap(), w_src.ap(), sem_stg[0])
            casts.append(dtok(sem_stg[0]))
        for i, w_dst in enumerate((wkT, wrT, wvT)):
            for cjc in range(CJ):
                emit("X", lambda x, i=i, cjc=cjc, w_dst=w_dst:
                     x.dma_start(out=w_dst[:, cjc * 1024:(cjc + 1) * 1024],
                                 in_=A_(wscr[i], cjc * 128,
                                        [[DM, DM], [1, 128]]),
                                 transpose=True),
                     waits=(casts[i],), dma_sem=sem_stg[1])
                key = sem_stg[1].name if hasattr(sem_stg[1], "name") else id(sem_stg[1])
                dma_count[key] = dma_count.get(key, 0) + 1
        t_ffnw = dtok(sem_stg[1])

    bank_free = {0: None, 1: None, 2: None, 3: None}
    ST = {
        "scan_prev": None, "prevcol": t_prev0, "xn2T_last": t_prev0,
        "z_done": [None] * NTC, "z_tiles": [None] * NTC,
        "store_all": [None] * (NTC + 1),
        "xT_free": [t_consts, t_consts],
        "vv_last": [None] * NTC, "kk_last": [None] * NTC,
        "smm": [None] * NTC, "rmm": [None] * NTC,
        "t_y0": [None] * NTC, "t_q": [None] * NTC, "t_p": [None] * NTC,
        "t_u": [None] * NTC, "t_evv": [None] * NTC,
        "t_xr": [None] * NTC, "t_xk": [None] * NTC, "t_diff": [None] * NTC,
        "t_xkl": [None] * NTC, "t_xrl": [None] * NTC, "t_d_ev": None,
        "t_x2l": [None] * NTC, "tka_last": [None] * NTC,
        "t_f": [None] * NTC,
        "r_done": [None] * NTC, "xbeta": [None] * NTC,
        "t_x2": [None] * NTC, "t_c2": [None] * NTC,
    }

    def stats_base(tc):
        par = (tc % 2) * 76
        return par + 0, par + 24, par + 48, par + 72

    def xtile(tc, ts):
        return xsb[:, (tc % 2) * 4096 + ts * 1024:
                   (tc % 2) * 4096 + (ts + 1) * 1024]

    def colmath_pair(base, ts0, SCR, eps, wa, wb, split_sum=False):
        def c(k, base=base, ts0=ts0):
            return stats[:, base + k + ts0:base + k + ts0 + 2]
        scr = stats[:, SCR + ts0:SCR + ts0 + 2]
        if split_sum:
            t0 = emit("V", lambda v, c=c: v.tensor_add(
                c(0), c(0), c(4)), waits=tuple(wa))
            wa = (t0,)
        t_a = emit("V", lambda v, c=c, scr=scr: v.tensor_mul(
            scr, c(0), c(0)), waits=tuple(wa))
        t3a = emit("V", lambda v, c=c: v.tensor_add(
            c(8), c(8), c(12)), waits=tuple(wb))
        t_b = emit("V", lambda v, c=c, eps=eps: v.tensor_scalar(
            out=c(8), in0=c(8), scalar1=INV_D, scalar2=eps,
            op0=ALU.mult, op1=ALU.add), waits=(t3a,))
        t_c = emit("V", lambda v, c=c, scr=scr: v.scalar_tensor_tensor(
            out=c(8), in0=scr, scalar=-INV_D * INV_D, in1=c(8),
            op0=ALU.mult, op1=ALU.add), waits=(t_a, t_b))
        t_d = emit("V", lambda v, c=c, scr=scr: v.reciprocal(scr, c(8)),
                   waits=(t_c,))
        t_e = emit("S", lambda s, c=c, scr=scr: s.activation(
            c(16), scr, AF.Sqrt), waits=(t_d,))
        t_f = emit("V", lambda v, c=c: v.scalar_tensor_tensor(
            out=c(20), in0=c(0), scalar=-INV_D, in1=c(16),
            op0=ALU.mult, op1=ALU.mult), waits=(t_e,))
        return t_f

    def ln_stats(tc, base, SCR, eps, src_tile, extra_red=(), extra_sq=()):
        t_cm = []
        for half in range(2):
            tred, tsq = [], []
            for ts in (2 * half, 2 * half + 1):
                tred.append(emit("V", lambda v, a=src_tile(ts),
                                 c=stats[:, base + ts:base + ts + 1]:
                                 v.reduce_sum(c, a, axis=AX.X),
                                 waits=tuple(extra_red)))
                for hh in range(2):
                    tsq.append(emit("S", lambda s, a=src_tile(ts), hh=hh,
                                    c=stats[:, base + 8 + 4 * hh + ts:
                                            base + 9 + 4 * hh + ts]:
                                    s.activation(
                                        sq[:, 0:512],
                                        bass.AP(a.tensor, a.offset + hh * 512,
                                                [[a.ap[0][0], 128], [1, 512]]),
                                        AF.Square, accum_out=c),
                                    waits=tuple(extra_sq)))
            t_cm.append(colmath_pair(base, 2 * half, SCR, eps,
                                     (tred[-1],), (tsq[-1],)))
        return t_cm

    def phase_front_a(tc):
        slot = tc % 2
        ST1, STB, STC, SCR = stats_base(tc)
        if tc >= 2:
            t_x[tc] = load_x(tc, waitl=ST["z_tiles"][tc - 2])

        t_ln1 = ln_stats(tc, ST1, SCR, EPS_PRE, lambda ts: xtile(tc, ts),
                         extra_red=(t_x[tc],), extra_sq=(t_x[tc],))
        t_c = []
        for ts in range(NTS):
            wl = [t_ln1[ts // 2]] if ts % 2 == 0 else []
            if ts == 0 and tc > 0:
                wl.append(ST["t_f"][tc - 1])
            t_c.append(emit("S", lambda s, a=xtile(tc, ts), ts=ts,
                            bi=stats[:, ST1 + 20 + ts:ST1 + 21 + ts],
                            sc=stats[:, ST1 + 16 + ts:ST1 + 17 + ts]:
                            s.activation(xnb[:, ts * 1024:(ts + 1) * 1024], a,
                                         AF.Identity, bias=bi, scale=sc),
                            waits=tuple(wl)))
        t_xbeta = emit("G", lambda g, xoff=slot * 4096: g.tensor_add(
            xsb[:, xoff:xoff + 4096], xsb[:, xoff:xoff + 4096],
            A_(bb, 0, [[DM, 128], [0, NTS], [1, DM]])),
            waits=(t_c[-1], t_ln1[-1], t_gb, t_x[tc]))
        ST["xbeta"][tc] = t_xbeta
        ST["t_c1"] = t_c

    def phase_front_b(tc):
        slot = tc % 2
        xTb = slot * 4096
        t_c = ST["t_c1"]
        t_d_ev = []
        for cj in range(CJ):
            rot = cj % 2
            ttr = None
            for ts in range(NTS):
                wl = [t_c[ts]]
                if ts == 0 and trbank_free[rot] is not None:
                    wl.append(trbank_free[rot])
                ttr = emit("T", lambda t, cj=cj, ts=ts, rot=rot: t.transpose(
                    p_tr[rot][:, ts * 128:(ts + 1) * 128],
                    xnb[:, ts * 1024 + cj * 128: ts * 1024 + (cj + 1) * 128],
                    eyeb[:]), waits=tuple(wl))
            tev = emit("V", lambda v, cj=cj, rot=rot, xTb=xTb: v.tensor_copy(
                xT[:, xTb + cj * 512: xTb + (cj + 1) * 512],
                p_tr[rot][:, 0:512]),
                waits=(ttr, ST["r_done"][tc - 2] if tc >= 2 else None))
            trbank_free[rot] = tev
            t_d_ev.append(tev)
        ST["t_d"] = t_d_ev[-1]
        ST["t_d_ev"] = t_d_ev

        qk_prev_reader = ST["t_y0"][tc - 1] if tc > 0 else None
        aug_prev = ST["t_q"][tc - 1] if tc > 0 else None
        for pi, (wsb, bcol, dstt, doff, twt) in enumerate(
                ((wq, bqc, qkbuf, 0, t_wq), (wk, bkc, xnb, 0, t_wkv),
                 (wv, bvc, vsb, 0, t_wkv))):
            for jc in range(CI):
                bk = 2 + (pi * CI + jc) % 2
                tmm = None
                for cj in range(CJ):
                    wl = [ST["t_d_ev"][cj]]
                    if cj == 0:
                        wl.append(twt)
                        if bank_free[bk] is not None:
                            wl.append(bank_free[bk])
                    tmm = emit("T", lambda t, wsb=wsb, jc=jc, cj=cj, bk=bk,
                               xTb=xTb: t.matmul(
                                   p_mm[bk][:, 0:512],
                                   wsb[:, cj * 512 + jc * 128:
                                       cj * 512 + (jc + 1) * 128],
                                   xT[:, xTb + cj * 512: xTb + (cj + 1) * 512],
                                   start=(cj == 0), stop=(cj == CJ - 1)),
                               waits=tuple(wl))
                wl = [tmm]
                if pi == 0 and jc == 0 and qk_prev_reader is not None:
                    wl.append(qk_prev_reader)
                if pi == 2 and jc == 0 and aug_prev is not None:
                    wl.append(aug_prev)
                if pi == 1 and jc == 0 and tc > 0:
                    wl.append(ST["tka_last"][tc - 1])
                tev = emit("V", lambda v, dstt=dstt, doff=doff, jc=jc,
                           bk=bk, bcol=bcol: v.tensor_scalar(
                               out=dstt[:, doff + jc * 512:
                                        doff + (jc + 1) * 512],
                               in0=p_mm[bk][:, 0:512],
                               scalar1=bcol[:, jc:jc + 1], scalar2=None,
                               op0=ALU.add), waits=tuple(wl))
                bank_free[bk] = tev
                if pi == 0:
                    ST["t_evq"] = tev
                elif pi == 1:
                    ST["t_evk"] = tev
                else:
                    ST["t_evv"][tc] = tev

    def phase_tail2(tc):
        nonlocal p_sm_free, p_st_free
        xTb = (tc % 2) * 4096
        if tc > 0:
            ST["prevcol"] = emit("V", lambda v: v.tensor_copy(
                A_(xn2T, 0, [[4104, 128], [513, CJ]]),
                A_(xn2T, 512, [[4104, 128], [513, CJ]])),
                waits=(ST["xn2T_last"],))
        wbank = [p_sm, p_st]
        wfree = [p_sm_free, p_st_free]
        t_w_ev = []
        for cj in range(CJ):
            rot = cj % 2
            ttr = None
            for ts in range(NTS):
                wl = [ST["t_c2"][tc][ts]]
                if ts == 0 and wfree[rot] is not None:
                    wl.append(wfree[rot])
                ttr = emit("T", lambda t, cj=cj, ts=ts, rot=rot: t.transpose(
                    wbank[rot][:, ts * 128:(ts + 1) * 128],
                    aA[:, ts * 1024 + cj * 128: ts * 1024 + (cj + 1) * 128],
                    eyef[:]), waits=tuple(wl))
            tev = emit("S", lambda s, cj=cj, rot=rot: s.activation(
                xn2T[:, cj * 513 + 1: cj * 513 + 513], wbank[rot][:, 0:512],
                AF.Identity), waits=(ttr, ST["prevcol"]))
            wfree[rot] = tev
            t_w_ev.append(tev)
        p_sm_free, p_st_free = wfree
        ST["xn2T_last"] = t_w_ev[-1]

        t_diff = emit("V", lambda v: v.tensor_sub(
            aA[:, 0:4096], A_(xn2T, 1, [[4104, 128], [513, CJ], [1, 512]]),
            A_(xn2T, 0, [[4104, 128], [513, CJ], [1, 512]])),
            waits=(t_w_ev[-1],))
        ST["t_diff"][tc] = t_diff
        t_xk = []
        for cj in range(CJ):
            wl = [t_diff, t_consts]
            if cj == 0:
                wl.append(ST["kk_last"][tc - 1] if tc > 0 else t_ffnw)
            t_xk.append(emit("V", lambda v, cj=cj: v.scalar_tensor_tensor(
                out=bigb[:, cj * 512:(cj + 1) * 512],
                in0=aA[:, cj * 512:(cj + 1) * 512],
                scalar=mukc[:, cj:cj + 1],
                in1=A_(xn2T, cj * 513, [[4104, 128], [1, 512]]),
                op0=ALU.mult, op1=ALU.add), waits=tuple(wl)))
        ST["t_xk"][tc] = t_xk[-1]
        ST["t_xkl"][tc] = t_xk
        t_xr = []
        for cj in range(CJ):
            wl = [t_diff, t_consts]
            if cj == 0:
                wl.append(ST["t_evv"][tc])
            t_xr.append(emit("V", lambda v, cj=cj, xTb=xTb:
                        v.scalar_tensor_tensor(
                            out=xT[:, xTb + cj * 512: xTb + (cj + 1) * 512],
                            in0=aA[:, cj * 512:(cj + 1) * 512],
                            scalar=murc[:, cj:cj + 1],
                            in1=A_(xn2T, cj * 513, [[4104, 128], [1, 512]]),
                            op0=ALU.mult, op1=ALU.add), waits=tuple(wl)))
        ST["t_xr"][tc] = t_xr[-1]
        ST["t_xrl"][tc] = t_xr

    def phase_mid_a1(tc):
        nonlocal p_sm_free, p_st_free
        vvp = ST["vv_last"][tc - 2] if tc >= 2 else None

        t_f = emit("V", lambda v: v.tensor_mul(
            bigb[:, 4096:6144], xnb[:, 0:2048], qkbuf[:, 0:2048]),
            waits=(ST["t_evq"], ST["t_evk"], vvp))
        ST["t_f"][tc] = t_f

        tmm = None
        for ts in range(NTS):
            for ci in range(CI):
                wl = []
                if ts == 0 and ci == 0:
                    wl = [t_f, t_consts]
                    if p_st_free is not None:
                        wl.append(p_st_free)
                tmm = emit("T", lambda t, ts=ts, ci=ci: t.matmul(
                    p_st[:, ts * 8:(ts + 1) * 8],
                    bigb[:, 4096 + ci * 512 + ts * 128:
                         4096 + ci * 512 + (ts + 1) * 128],
                    sels[:, ci * 8:(ci + 1) * 8],
                    start=(ci == 0), stop=(ci == CI - 1)), waits=tuple(wl))
        ST["smm"][tc] = tmm
        t_j = emit("V", lambda v: v.reduce_max(
            mx4[:], A_(p_st, 0, [[512, 128], [8, 4], [1, 8]]), axis=AX.X,
            negate=True), waits=(tmm,))
        texp = None
        for g in range(4):
            texp = emit("S", lambda s, g=g: s.activation(
                stm[:, g * 8:(g + 1) * 8], p_st[:, g * 8:(g + 1) * 8],
                AF.Exp, bias=mx4[:, g:g + 1]),
                waits=(t_j,) if g == 0 else ())
        p_st_free = texp
        t_j = emit("V", lambda v: v.reduce_sum(
            rc4[:], A_(stm, 0, [[32, 128], [8, 4], [1, 8]]), axis=AX.X),
            waits=(texp,))
        t_j = emit("V", lambda v: v.reciprocal(rc4[:], rc4[:]), waits=(t_j,))
        t_j = emit("V", lambda v: v.tensor_mul(
            A_(stm, 0, [[32, 128], [8, 4], [1, 8]]),
            A_(stm, 0, [[32, 128], [8, 4], [1, 8]]),
            A_(rc4, 0, [[4, 128], [1, 4], [0, 8]])), waits=(t_j,))
        ST["t_jn"] = t_j
        ST["texp"] = texp
        p_st_free = texp

    def phase_mid_a2(tc):
        nonlocal p_sm_free, p_st_free
        t_j = ST["t_jn"]
        ttr = None
        for ts in range(NTS):
            wl = []
            if ts == 0:
                wl = [t_j]
                if p_sm_free is not None:
                    wl.append(p_sm_free)
            ttr = emit("T", lambda t, ts=ts: t.transpose(
                p_sm[0:8, ts * 128:(ts + 1) * 128], stm[:, ts * 8:(ts + 1) * 8],
                eyef[:]), waits=tuple(wl))
        p_sm_free = emit("V", lambda v: v.tensor_copy(arow[:],
                                                      p_sm[0:8, 0:512]),
                         waits=(ttr,))
        t_l = p_sm_free

        t_u = []
        for ci in range(CI):
            bk = 2 + ci % 2
            wl = [t_l, t_consts]
            if bank_free[bk] is not None:
                wl.append(bank_free[bk])
            tmm = emit("T", lambda t, ci=ci, bk=bk: t.matmul(
                p_mm[bk][:, 0:512], selb[0:8, ci * 128:(ci + 1) * 128],
                arow[0:8, 0:512], start=True, stop=True), waits=tuple(wl))
            wl = [tmm, ST["t_evv"][tc], t_decay]
            if ci == 0 and tc > 0:
                wl.append(ST["t_q"][tc - 1])
            tn = emit("V", lambda v, ci=ci, bk=bk: v.scalar_tensor_tensor(
                out=aB[:, ci * 512:(ci + 1) * 512], in0=p_mm[bk][:, 0:512],
                scalar=omd[:, ci:ci + 1], in1=vsb[:, ci * 512:(ci + 1) * 512],
                op0=ALU.mult, op1=ALU.mult), waits=tuple(wl))
            bank_free[bk] = tn
            t_u.append(tn)
        ST["t_u"][tc] = t_u[-1]

        t_scan = []
        for ci in range(CI):
            wl = [t_u[ci]]
            if tc == 0 and ci == 0:
                wl.append(t_carry0)
            t_scan.append(emit("V", lambda v, ci=ci: v.tensor_tensor_scan(
                out=aB[:, ci * 512:(ci + 1) * 512],
                data0=A_(decay, ci, [[4, 128], [0, 512]]),
                data1=aB[:, ci * 512:(ci + 1) * 512],
                initial=carry[:, ci:ci + 1], op0=ALU.mult, op1=ALU.add),
                waits=tuple(wl)))
        ST["scan_prev"] = emit("V", lambda v: v.tensor_copy(
            carry[:, 0:4], A_(aB, 511, [[2048, 128], [512, 4]])),
            waits=(t_scan[-1],))

        t_p = []
        for ci in range(CI):
            t_p.append(emit("S", lambda s, ci=ci: s.activation(
                vsb[:, ci * 512:(ci + 1) * 512],
                qkbuf[:, ci * 512:(ci + 1) * 512],
                AF.Tanh, scale=augph[:, ci:ci + 1]),
                waits=(t_augph, t_u[ci])))
        ST["t_p"][tc] = t_p[-1]
        t_y0 = emit("V", lambda v: v.tensor_mul(
            aB[:, 0:2048], aB[:, 0:2048], qkbuf[:, 0:2048]),
            waits=(ST["scan_prev"],))
        ST["t_y0"][tc] = t_y0
    def phase_mid_b(tc):
        t_q = emit("V", lambda v: v.scalar_tensor_tensor(
            out=bigb[:, 6144:8192], in0=vsb[:, 0:2048], scalar=0.5,
            in1=aB[:, 0:2048], op0=ALU.mult, op1=ALU.add),
            waits=(ST["t_p"][tc], ST["t_y0"][tc],
                   ST["vv_last"][tc - 1] if tc > 0 else None))
        ST["t_q"][tc] = t_q

    def phase_mid_r(tc):
        ST1, STB, STC, SCR = stats_base(tc)
        t_q = ST["t_q"][tc]
        t_oev = []
        for ts in range(NTS):
            for jh in range(2):
                bk = (ts * 2 + jh) % 4
                tmm = None
                for ic in range(CI):
                    wl = []
                    if ic == 0:
                        wl = [t_q, t_bff]
                        if bank_free[bk] is not None:
                            wl.append(bank_free[bk])
                    tmm = emit("T", lambda t, ts=ts, jh=jh, ic=ic, bk=bk:
                               t.matmul(p_mm[bk][:, 0:512],
                                        bigb[:, 6144 + ic * 512 + ts * 128:
                                             6144 + ic * 512 + (ts + 1) * 128],
                                        wo[:, ic * 1024 + jh * 512:
                                           ic * 1024 + (jh + 1) * 512],
                                        start=(ic == 0), stop=False),
                               waits=tuple(wl))
                tmm = emit("T", lambda t, jh=jh, bk=bk: t.matmul(
                    p_mm[bk][:, 0:512], ones1b[0:1, 0:128],
                    bffr[0:1, jh * 512:(jh + 1) * 512], start=False, stop=True))
                wl = [tmm]
                if ts == 0 and jh == 0 and ST["store_all"][tc - 1] is not None:
                    wl.append(ST["store_all"][tc - 1])
                tev = emit("S", lambda s, ts=ts, jh=jh, bk=bk,
                           c=stats[:, STB + 4 * jh + ts:STB + 1 + 4 * jh + ts]:
                           s.activation(
                               aA[:, ts * 1024 + jh * 512:
                                  ts * 1024 + (jh + 1) * 512],
                               p_mm[bk][:, 0:512], AF.Identity, accum_out=c),
                           waits=tuple(wl))
                bank_free[bk] = tev
                t_oev.append(tev)
        ST["rmm"][tc] = tmm
        ST["t_oev"] = t_oev

    def phase_tail1_s1(tc, half):
        ST1, STB, STC, SCR = stats_base(tc)
        t_oev = ST["t_oev"]
        if half == 0:
            ST["t_c2"][tc] = [None] * NTS
            ST["t_x2l"][tc] = [None] * NTS
            ST["t_t1"] = [None] * NTS
        p0 = 2 * half
        tsq = []
        for ts in (p0, p0 + 1):
            for hh in range(2):
                tsq.append(emit("S", lambda s, ts=ts, hh=hh,
                                c=stats[:, STB + 8 + 4 * hh + ts:
                                        STB + 9 + 4 * hh + ts]:
                                s.activation(
                                    sq[:, 0:512],
                                    aA[:, ts * 1024 + hh * 512:
                                       ts * 1024 + (hh + 1) * 512],
                                    AF.Square, accum_out=c),
                                waits=(t_oev[2 * ts + hh],)))
        t_lnb = colmath_pair(STB, p0, SCR, EPS_ATTN,
                             (t_oev[2 * p0 + 3],), (tsq[-1],),
                             split_sum=True)
        for ts in (p0, p0 + 1):
            ST["t_t1"][ts] = emit("S", lambda s, ts=ts,
                                  bi=stats[:, STB + 20 + ts:STB + 21 + ts],
                                  sc=stats[:, STB + 16 + ts:STB + 17 + ts]:
                                  s.activation(
                                      aA[:, ts * 1024:(ts + 1) * 1024],
                                      aA[:, ts * 1024:(ts + 1) * 1024],
                                      AF.Identity, bias=bi, scale=sc),
                                  waits=(t_lnb,) if ts == p0 else ())

    def phase_tail1_s2(tc, half):
        ST1, STB, STC, SCR = stats_base(tc)
        t_c2 = ST["t_c2"][tc]
        x2_all = ST["t_x2l"][tc]
        t_x2 = None
        p0 = 2 * half
        x2l = {}
        for ts in (p0, p0 + 1):
            tg = emit("V", lambda v, ts=ts: v.tensor_mul(
                aA[:, ts * 1024:(ts + 1) * 1024],
                aA[:, ts * 1024:(ts + 1) * 1024], gb[:, 0:DM]),
                waits=(ST["t_t1"][ts], t_gb))
            t_x2 = emit("V", lambda v, a=xtile(tc, ts), ts=ts,
                        c=stats[:, STC + ts:STC + ts + 1]:
                        v.scalar_tensor_tensor(
                            out=a, in0=aA[:, ts * 1024:(ts + 1) * 1024],
                            scalar=1.0, in1=a, op0=ALU.mult, op1=ALU.add,
                            accum_out=c),
                        waits=(tg, ST["xbeta"][tc]))
            x2l[ts] = t_x2
            x2_all[ts] = t_x2
        tsq = []
        for ts in (p0, p0 + 1):
            for hh in range(2):
                tsq.append(emit("S", lambda s, ts=ts, hh=hh,
                                a=xtile(tc, ts),
                                c=stats[:, STC + 8 + 4 * hh + ts:
                                        STC + 9 + 4 * hh + ts]:
                                s.activation(
                                    sq[:, 0:512],
                                    bass.AP(a.tensor, a.offset + hh * 512,
                                            [[a.ap[0][0], 128], [1, 512]]),
                                    AF.Square, accum_out=c),
                                waits=(x2l[ts],)))
        t_lnc = colmath_pair(STC, p0, SCR, EPS_PRE,
                             (t_x2,), (tsq[-1],))
        for ts in (p0, p0 + 1):
            t_c2[ts] = emit("S", lambda s, a=xtile(tc, ts), ts=ts,
                            bi=stats[:, STC + 20 + ts:STC + 21 + ts],
                            sc=stats[:, STC + 16 + ts:STC + 17 + ts]:
                            s.activation(
                                aA[:, ts * 1024:(ts + 1) * 1024], a,
                                AF.Identity, bias=bi, scale=sc),
                            waits=(t_lnc,) if ts == p0 else ())
        ST["t_x2"][tc] = t_x2

    def phase_ffn(tc):
        xTb = (tc % 2) * 4096
        t_ka = []
        trl_last = None
        for dj in range(CJ):
            bk = dj % 4
            tmm = None
            for cjc in range(CJ):
                wl = [ST["t_xkl"][tc][cjc]]
                if cjc == 0:
                    wl.append(t_ffnw)
                    if bank_free[bk] is not None:
                        wl.append(bank_free[bk])
                tmm = emit("T", lambda t, dj=dj, cjc=cjc, bk=bk: t.matmul(
                    p_mm[bk][:, 0:512],
                    wkT[:, cjc * 1024 + dj * 128: cjc * 1024 + (dj + 1) * 128],
                    bigb[:, cjc * 512:(cjc + 1) * 512],
                    start=(cjc == 0), stop=(cjc == CJ - 1)), waits=tuple(wl))
            trl = emit("V", lambda v, bk=bk, dj=dj: v.tensor_scalar_max(
                xnb[:, (dj % 4) * 512:(dj % 4 + 1) * 512], p_mm[bk][:, 0:512],
                0.0), waits=(tmm, ST["t_f"][min(tc + 1, NTC - 1)]))
            trl_last = trl
            wl = [trl]
            if dj < 4:
                wl.append(ST["smm"][min(tc + 1, NTC - 1)])
            else:
                wl.append(ST["rmm"][tc])
            tka = emit("S", lambda s, dj=dj: s.activation(
                bigb[:, 4096 + dj * 512: 4096 + (dj + 1) * 512],
                xnb[:, (dj % 4) * 512:(dj % 4 + 1) * 512], AF.Square),
                waits=tuple(wl))
            bank_free[bk] = tka
            t_ka.append(tka)
        kk_mm_last = tmm
        ST["tka_last"][tc] = t_ka[-1]

        ST["kk_last"][tc] = kk_mm_last

        last_tz = None
        tz3 = None
        z_tiles = []
        t_sg = []
        for ts in range(NTS):
            t_tr = []
            for dh in range(2):
                bk = 2 + (ts * 2 + dh) % 2
                tmm = None
                for cjc in range(CJ):
                    wl = [ST["t_xrl"][tc][cjc]]
                    if cjc == 0:
                        wl.append(t_ffnw)
                        if bank_free[bk] is not None:
                            wl.append(bank_free[bk])
                    tmm = emit("T", lambda t, ts=ts, dh=dh, cjc=cjc, bk=bk,
                               xTb=xTb: t.matmul(
                                   p_mm[bk][:, 0:512],
                                   xT[:, xTb + cjc * 512 + ts * 128:
                                      xTb + cjc * 512 + (ts + 1) * 128],
                                   wrT[:, cjc * 1024 + dh * 512:
                                       cjc * 1024 + (dh + 1) * 512],
                                   start=(cjc == 0), stop=(cjc == CJ - 1)),
                               waits=tuple(wl))
                wl = [tmm, ST["t_xk"][tc], ST["t_diff"][tc],
                      ST["t_c2"][tc][-1]]
                ttr2 = emit("S", lambda s, ts=ts, dh=dh, bk=bk: s.activation(
                    aA[:, ts * 1024 + dh * 512: ts * 1024 + (dh + 1) * 512],
                    p_mm[bk][:, 0:512], AF.Tanh, scale=0.5), waits=tuple(wl))
                bank_free[bk] = ttr2
                t_tr.append(ttr2)
            sg = emit("S", lambda s, ts=ts: s.activation(
                aA[:, ts * 1024:(ts + 1) * 1024],
                aA[:, ts * 1024:(ts + 1) * 1024],
                AF.Identity, bias=halfc[:, 0:1], scale=halfc[:, 0:1]),
                waits=(t_tr[1], t_half))
            t_sg.append(sg)
            for ih in range(2):
                bk = (ts * 2 + ih) % 2
                tmm = None
                for dj in range(CJ):
                    wl = []
                    if dj == 0:
                        wl = [t_ka[-1], t_ffnw]
                        if bank_free[bk] is not None:
                            wl.append(bank_free[bk])
                    tmm = emit("T", lambda t, ts=ts, ih=ih, dj=dj, bk=bk:
                               t.matmul(
                                   p_mm[bk][:, 0:512],
                                   bigb[:, 4096 + dj * 512 + ts * 128:
                                        4096 + dj * 512 + (ts + 1) * 128],
                                   wvT[:, dj * 1024 + ih * 512:
                                       dj * 1024 + (ih + 1) * 512],
                                   start=(dj == 0), stop=(dj == CJ - 1)),
                               waits=tuple(wl))
                ST["vv_last"][tc] = tmm
                tz = emit("V", lambda v, ts=ts, ih=ih, bk=bk: v.tensor_mul(
                    aA[:, ts * 1024 + ih * 512: ts * 1024 + (ih + 1) * 512],
                    p_mm[bk][:, 0:512],
                    aA[:, ts * 1024 + ih * 512: ts * 1024 + (ih + 1) * 512]),
                    waits=(tmm, sg))
                bank_free[bk] = tz
                last_tz = tz
            tz3 = emit("V", lambda e, ts=ts, a=xtile(tc, ts): e.tensor_add(
                aA[:, ts * 1024:(ts + 1) * 1024],
                aA[:, ts * 1024:(ts + 1) * 1024], a),
                waits=(last_tz, ST["t_x2l"][tc][ts]))
            z_tiles.append(tz3)
            out_ap = A_(y_d, (tc * TCW + ts * 128) * DM, [[DM, 128], [1, DM]])
            dma("X", out_ap, aA[:, ts * 1024:(ts + 1) * 1024], sem_do,
                waits=(tz3,))
        ST["r_done"][tc] = t_sg[-1]
        ST["z_done"][tc] = tz3
        ST["z_tiles"][tc] = z_tiles
        ST["store_all"][tc] = dtok(sem_do)

    phase_front_a(0)
    emit_ffn_weights()
    phase_front_b(0)
    phase_mid_a1(0)
    phase_mid_a2(0)
    for tc in range(NTC):
        if tc > 0:
            phase_tail2(tc - 1)
            phase_ffn(tc - 1)
        phase_mid_b(tc)
        phase_mid_r(tc)
        if tc + 1 < NTC:
            phase_front_a(tc + 1)
            phase_front_b(tc + 1)
        phase_tail1_s1(tc, 0)
        phase_tail1_s1(tc, 1)
        if tc + 1 < NTC:
            phase_mid_a1(tc + 1)
        phase_tail1_s2(tc, 0)
        phase_tail1_s2(tc, 1)
        if tc + 1 < NTC:
            phase_mid_a2(tc + 1)
    phase_tail2(NTC - 1)
    phase_ffn(NTC - 1)

    emit("X", None, waits=(dtok(sem_do),))

    marks = {e: set() for e in ENGS}
    for e in ENGS:
        for ent in sched[e]:
            for w in ent["waits"]:
                if w[0] == "E":
                    marks[w[1]].add(w[2])
    mile = {}
    for e in ENGS:
        mile[e] = {idx: n + 1 for n, idx in enumerate(sorted(marks[e]))}

    def runner(e):
        def run(eng):
            waited = {}
            for idx, ent in enumerate(sched[e]):
                for w in ent["waits"]:
                    if w[0] == "E":
                        _, pe, pidx = w
                        val = mile[pe][pidx]
                        key = "E" + pe
                        if waited.get(key, 0) < val:
                            eng.wait_ge(esem[pe], val)
                            waited[key] = val
                    else:
                        _, dsem, val = w
                        key = dsem.name if hasattr(dsem, "name") else id(dsem)
                        if waited.get(("D", key), 0) < val:
                            eng.wait_ge(dsem, val)
                            waited[("D", key)] = val
                if ent["fn"] is None:
                    continue
                inst = ent["fn"](eng)
                if ent["dma_sem"] is not None:
                    inst = inst.then_inc(ent["dma_sem"], 16)
                if idx in mile[e]:
                    inst.then_inc(esem[e], 1)
        return run

    with es:
        with nc.Block() as blk:
            blk.tensor(runner("T"))
            blk.vector(runner("V"))
            blk.scalar(runner("S"))
            blk.gpsimd(runner("G"))
            blk.sync(runner("X"))
    return nc


_NC_CACHE = None


def kernel(L0_attn_x, L0_attn_Wq, L0_attn_bq, L0_attn_Wk, L0_attn_bk,
           L0_attn_Wv, L0_attn_bv, L0_attn_Wo, L0_attn_bo, L0_attn_alpha,
           L0_attn_aug_param, L0_attn_ln_scale, L0_attn_ln_bias,
           L0_ffn_W_r, L0_ffn_W_k, L0_ffn_W_v, L0_ffn_mu_r, L0_ffn_mu_k):
    global _NC_CACHE
    if _NC_CACHE is None:
        _NC_CACHE = build_nc()
    nc = _NC_CACHE

    c = np.ascontiguousarray
    f32 = np.float32

    def cols(vec, k):
        return c(np.asarray(vec, f32).reshape(k, 128).T)

    base = {
        "wq": c(np.asarray(L0_attn_Wq, f32)),
        "wk": c(np.asarray(L0_attn_Wk, f32)),
        "wv": c(np.asarray(L0_attn_Wv, f32)),
        "wo": c(np.asarray(L0_attn_Wo, f32)),
        "fr": c(np.asarray(L0_ffn_W_r, f32)),
        "fk": c(np.asarray(L0_ffn_W_k, f32)),
        "fv": c(np.asarray(L0_ffn_W_v, f32)),
        "bqc": cols(L0_attn_bq, 4),
        "bkc": cols(L0_attn_bk, 4),
        "bvc": cols(L0_attn_bv, 4),
        "alc": cols(np.asarray(L0_attn_alpha, f32).reshape(512), 4),
        "apc": cols(np.asarray(L0_attn_aug_param, f32).reshape(512), 4),
        "murc": cols(L0_ffn_mu_r, 8),
        "mukc": cols(L0_ffn_mu_k, 8),
        "grow": c(np.asarray(L0_attn_ln_scale, f32).reshape(1, DM)),
        "brow": c(np.asarray(L0_attn_ln_bias, f32).reshape(1, DM)),
        "borow": c(np.asarray(L0_attn_bo, f32).reshape(1, DM)),
        "eyeb": np.eye(128, dtype=BFnp),
        "eyef": np.eye(128, dtype=f32),
        "ones1b": np.ones((1, 128), BFnp),
        "ones1f": np.ones((1, 128), f32),
        "onescb": np.ones((128, 1), BFnp),
    }
    sels = np.zeros((128, 32), f32)
    selb = np.zeros((8, 512), f32)
    for ci in range(4):
        for p in range(128):
            h = 2 * ci + (1 if p >= 64 else 0)
            sels[p, ci * 8 + h] = 0.125
            selb[h, ci * 128 + p] = 1.0
    base["sels"] = sels.astype(BFnp)
    base["selb"] = selb

    xs = np.asarray(L0_attn_x, f32)
    in_maps = []
    for b in range(8):
        m = dict(base)
        m["x"] = c(xs[b])
        in_maps.append(m)

    res = run_bass_kernel_spmd(nc, in_maps, core_ids=list(range(8)))
    out = np.stack([np.asarray(r["y"]) for r in res.results], axis=0)
    return out.astype(np.float32)
